# version 27
# baseline (speedup 1.0000x reference)
import numpy as np

import concourse.bass as bass
import concourse.mybir as mybir
from concourse.bass_utils import run_bass_kernel_spmd

F32 = mybir.dt.float32
F16 = mybir.dt.float16
BF16 = mybir.dt.bfloat16
U8 = mybir.dt.uint8
AF = mybir.ActivationFunctionType
ALU = mybir.AluOpType

DM, DI, NS, DR = 1024, 2048, 16, 64
T, TH, NJ = 2048, 1024, 512
KC, DC, GRP = 8, 16, 4
EPS = 1e-5
TAPS = []
EMIT_LOG = None
ESZ = {F32: 4, F16: 2, BF16: 2, U8: 1}


class Mem:
    def __init__(self, total, start=0):
        self.total, self.cur = total, start

    def alloc(self, name, cols, dt_):
        off = -(-self.cur // 32) * 32
        self.cur = off + cols * ESZ[dt_]
        assert self.cur <= self.total, f"SBUF overflow at {name}: {self.cur}"
        return off


def build_program():
    nc = bass.Bass("TRN2", target_bir_lowering=False)

    dx = nc.dram_tensor("x_t", [DM, T], F32, kind="ExternalInput").ap()
    dWin = nc.dram_tensor("w_in_t", [DM, 4192], F32, kind="ExternalInput").ap()
    dWdt = nc.dram_tensor("w_dt_t", [DR, DI], F32, kind="ExternalInput").ap()
    dWout = nc.dram_tensor("w_out_t", [DI, DM], F32, kind="ExternalInput").ap()
    dWr = nc.dram_tensor("w_r_t", [DM, DM], F32, kind="ExternalInput").ap()
    dWk = nc.dram_tensor("w_k_t", [DM, DM], F32, kind="ExternalInput").ap()
    dWv = nc.dram_tensor("w_v_t", [DM, DM], F32, kind="ExternalInput").ap()
    dAlog = nc.dram_tensor("a_log_t", [128, DC * NS], F32, kind="ExternalInput").ap()
    dD = nc.dram_tensor("dd_t", [128, DC], F32, kind="ExternalInput").ap()
    dbdt = nc.dram_tensor("bdt_t", [128, DC], F32, kind="ExternalInput").ap()
    dcw = nc.dram_tensor("convw_t", [128, DC * 4], F32, kind="ExternalInput").ap()
    dcb = nc.dram_tensor("convb_t", [128, DC], F32, kind="ExternalInput").ap()
    dlng = nc.dram_tensor("lng_t", [128, KC], F32, kind="ExternalInput").ap()
    dlnb = nc.dram_tensor("lnb_t", [128, KC], F32, kind="ExternalInput").ap()
    dmur = nc.dram_tensor("mur_t", [128, KC], F32, kind="ExternalInput").ap()
    dmuk = nc.dram_tensor("muk_t", [128, KC], F32, kind="ExternalInput").ap()
    dy = nc.dram_tensor("y_t", [DM, T], F32, kind="ExternalOutput").ap()

    sWin = nc.dram_tensor("s_win", [DM, 4192], BF16, kind="Internal").ap()
    sWout = nc.dram_tensor("s_wout", [DI, DM], BF16, kind="Internal").ap()
    sWr = nc.dram_tensor("s_wr", [DM, DM], BF16, kind="Internal").ap()
    sWk = nc.dram_tensor("s_wk", [DM, DM], BF16, kind="Internal").ap()
    sWv = nc.dram_tensor("s_wv", [DM, DM], BF16, kind="Internal").ap()
    sXbf = nc.dram_tensor("s_xbf", [DM, T], BF16, kind="Internal").ap()
    sBC = nc.dram_tensor("s_bc", [32, TH], F16, kind="Internal").ap()
    sDiag = nc.dram_tensor("s_diag", [DC * 4 * 128, 128], BF16, kind="Internal").ap()

    TOT = 206 * 1024
    mem = Mem(TOT)
    o_yg = mem.alloc("y_g", DC * TH, BF16)
    o_A = mem.alloc("A", DC * NS, F32)
    o_D = mem.alloc("D", DC, F32)
    o_bdt = mem.alloc("bdt", DC, F32)
    o_cw = mem.alloc("convw", DC * 4, F32)
    o_cwb = mem.alloc("convwb", DC * 4, BF16)
    o_cb = mem.alloc("convb", DC, F32)
    o_lng = mem.alloc("lng", KC, F32)
    o_lnb = mem.alloc("lnb", KC, F32)
    o_mur = mem.alloc("mur", KC, F32)
    o_muk = mem.alloc("muk", KC, F32)
    o_onesb = mem.alloc("onesb", 1, BF16)
    o_bcl = mem.alloc("bcl", 128, F32)
    o_carry = mem.alloc("carry", DC * NS, F32)
    o_wdt = mem.alloc("wdt_bf", DI, BF16)
    o_dtin = mem.alloc("dtin_bf", TH, BF16)
    o_wbcd = mem.alloc("wbcd", KC * 96, BF16)
    o_x1c = mem.alloc("x1_carry", KC, F32)
    o_xsc = mem.alloc("xs_carry", DC * 4, F16)
    o_bcsb = mem.alloc("bc_sb", TH, F16)
    o_row1 = mem.alloc("row_1", NJ, F32)
    o_row2 = mem.alloc("row_2", NJ, F32)
    o_row3 = mem.alloc("row_3", NJ, F32)
    o_ceps = mem.alloc("c_eps", 1, F32)
    o_cmh = mem.alloc("c_mh", 1, F32)
    o_cm1 = mem.alloc("c_m1", 1, F32)
    o_c1 = mem.alloc("c_1", 1, F32)
    PERS = mem.cur

    m2 = Mem(TOT, PERS)
    o_dt = [m2.alloc(f"dt{i}", TH, F16) for i in range(2 * GRP)]
    o_u = [m2.alloc(f"u{i}", TH, F16) for i in range(2 * GRP)]
    o_dtu = [m2.alloc(f"dtu{i}", TH, F16) for i in range(2 * GRP)]
    o_sz = [m2.alloc(f"sz{i}", TH, F16) for i in range(2 * GRP)]
    o_acc = [m2.alloc(f"acc{i}", TH, F16) for i in range(GRP)]
    o_da = [m2.alloc(f"da{i}", TH, F16) for i in range(6)]
    o_bm = [m2.alloc(f"bm{i}", TH, F16) for i in range(1)]
    o_ptmp = o_bm[0]
    o_yfin = o_bm[0]
    o_hh = [m2.alloc(f"h{i}", TH, F16) for i in range(4)]
    o_ymE = [m2.alloc(f"ymE{i}", TH, F16) for i in range(GRP)]
    o_ymT = m2.alloc("ymT", TH, F16)
    o_brep = [m2.alloc(f"brep{i}", TH, F16) for i in range(2)]
    o_crep = [m2.alloc(f"crep{i}", TH, F16) for i in range(2)]
    o_xs = [m2.alloc(f"xs{i}", TH + 4, BF16) for i in range(2)]
    o_wzx = [m2.alloc(f"wzx{i}", 2 * KC * 128, BF16) for i in range(2)]
    o_dg = [m2.alloc(f"dg{i}", 4 * 128, BF16) for i in range(2)]
    o_sp1 = m2.alloc("sp1", NJ, F32)
    o_sp2 = [m2.alloc(f"sp2{i}", NJ, F16) for i in range(2)]
    o_sp3 = [m2.alloc(f"sp3{i}", NJ, F16) for i in range(2)]
    o_xbh = m2.alloc("x_bh", KC * TH, BF16)
    S2_END = m2.cur

    m3 = Mem(TOT, PERS)
    o_sa_ = m3.alloc("streamA", KC * NJ, F32)
    o_x1 = m3.alloc("x1", KC * NJ, F32)
    o_vbf = m3.alloc("vbf", KC * NJ, BF16)
    o_sqbf = m3.alloc("sqbf", KC * NJ, BF16)
    o_rbf = m3.alloc("rbf", KC * NJ, BF16)
    o_k2bf = m3.alloc("k2bf", KC * NJ, BF16)
    o_xr = m3.alloc("xr", KC * NJ, BF16)
    o_xk = m3.alloc("xk", KC * NJ, BF16)
    o_kr = m3.alloc("kr", NJ, BF16)
    o_zt = m3.alloc("ztmp", NJ, F32)
    o_x1e = m3.alloc("x1e", NJ + 1, F32)
    o_xfull = m3.alloc("xfull", KC * NJ, F32)
    o_wst = [m3.alloc(f"wst{i}", 16 * 128, BF16) for i in range(2)]
    S3_END = m3.cur

    mt = Mem(TOT, 0)
    mt.cur = 0
    m0 = Mem(TOT, PERS)
    o_stg2 = [m0.alloc(f"stg{i}", 4192, F32) for i in range(2)]
    o_stgb2 = [m0.alloc(f"stgb{i}", 4192, BF16) for i in range(2)]
    o_stg = o_stg2[0]
    o_stgb = o_stgb2[0]
    S0_END = m0.cur
    mt.cur = max(S2_END, S3_END, S0_END)
    o_stg3 = [mt.alloc(f"stg3_{i}", 512, F32) for i in range(2)]
    o_stgb3 = [mt.alloc(f"stgb3_{i}", 512, BF16) for i in range(2)]
    assert mt.cur <= TOT, (S2_END, S3_END, S0_END, mt.cur)

    sched = []

    with (
        nc.sbuf_tensor("big", [128, TOT], U8) as big,
        nc.psum_tensor("ps_a", [128, 512], F32) as ps_a,
        nc.psum_tensor("ps_b", [128, 512], F32) as ps_b,
        nc.psum_tensor("ps_c", [128, 512], F32) as ps_c,
        nc.psum_tensor("ps_d", [128, 512], F32) as ps_d,
        nc.psum_tensor("ps_e", [128, 512], F32) as ps_e,
        nc.psum_tensor("ps_f", [128, 512], F32) as ps_f,
        nc.psum_tensor("ps_g", [128, 512], F32) as ps_g,
        nc.psum_tensor("ps_h", [128, 512], F32) as ps_h,
        nc.semaphore("sd") as sd,
        nc.semaphore("sg") as sg,
        nc.semaphore("spe") as spe,
        nc.semaphore("sa") as sa,
        nc.semaphore("sv") as sv,
        nc.semaphore("sp") as sp,
        nc.semaphore("sc") as sc,
        nc.Block() as blk,
    ):
        SEM = {"sd": sd, "sg": sg, "spe": spe, "sa": sa, "sv": sv, "sp": sp,
               "sc": sc}
        CNT = {k: 0 for k in SEM}
        STEP = {"sd": 16, "sg": 16, "spe": 1, "sa": 1, "sv": 1, "sp": 1,
                "sc": 16}
        ENG_SEM = {"sync": "sd", "gpsimd_dma": "sg", "tensor": "spe",
                   "scalar": "sa", "vector": "sv", "gpsimd": "sp",
                   "scalar_dma": "sc"}

        def V(off, cols, dt_, p0=0, p1=128):
            return big[p0:p1, off:off + cols * ESZ[dt_]].bitcast(dt_)

        def R1(off, p0=0):
            return V(off, NJ, F32, p0, p0 + 1)

        res = {}

        def _wait_list(reads, writes):
            need = {}
            for nm in reads:
                st = res.get(nm)
                if st and st["w"]:
                    s, v = st["w"]
                    need[s] = max(need.get(s, 0), v)
            for nm in writes:
                st = res.get(nm)
                if st:
                    if st["w"]:
                        s, v = st["w"]
                        need[s] = max(need.get(s, 0), v)
                    for s, v in st["r"]:
                        need[s] = max(need.get(s, 0), v)
            return sorted(need.items())

        def emit(eng, body, reads=(), writes=(), extra=(), last=True):
            sem_name = ENG_SEM[eng]
            waits = _wait_list(reads, writes)
            extra = [(s, v) for s, v in extra if v > 0]
            if EMIT_LOG is not None:
                EMIT_LOG.append((eng, ENG_SEM[eng], CNT[ENG_SEM[eng]] + STEP[ENG_SEM[eng]],
                                 tuple(waits), tuple(reads), tuple(writes)))
            CNT[sem_name] += STEP[sem_name]
            val = CNT[sem_name]
            eng_stream = {"gpsimd_dma": "gpsimd",
                          "scalar_dma": "scalar"}.get(eng, eng)

            def fn(e, body=body, waits=tuple(waits), extra=tuple(extra),
                   sem_name=sem_name):
                for s, v in waits:
                    e.wait_ge(SEM[s], v)
                for s, v in extra:
                    e.wait_ge(SEM[s], v)
                r = body(e)
                inc = STEP[sem_name]
                r.then_inc(SEM[sem_name], inc)
            sched.append((eng_stream, fn))
            ev = (sem_name, val)
            for nm in writes:
                res[nm] = {"w": ev, "r": []}
            for nm in reads:
                st = res.setdefault(nm, {"w": None, "r": []})
                st["r"].append(ev)
            return ev

        def raw_wait(eng, ev):
            def fn(e, ev=ev):
                e.wait_ge(SEM[ev[0]], ev[1])
            sched.append((eng, fn))

        tap_ap = {}

        def tap(name, view, reads):
            if name not in TAPS:
                return
            if name not in tap_ap:
                tap_ap[name] = nc.dram_tensor(
                    "tap_" + name, list(view.shape), view.dtype,
                    kind="ExternalOutput").ap()
            emit("sync", lambda e, view=view, name=name: e.dma_start(
                out=tap_ap[name], in_=view), reads=list(reads))

        OVL = (["stg", "stgb", "stgx0", "stgx1"]
               + [f"dt{i}_{j}" for i in range(2 * GRP) for j in range(2)]
               + [f"u{i}_{j}" for i in range(2 * GRP) for j in range(2)]
               + [f"sz{i}_{j}" for i in range(2 * GRP) for j in range(2)]
               + [f"dtu{i}" for i in range(2 * GRP)]
               + [f"accT{i}" for i in range(GRP)]
               + [f"ymE{i}" for i in range(GRP)] + ["ymT"]
               + [f"da{i}" for i in range(4)] + ["bm0"]
               + [f"hh{i}" for i in range(2)]
               + [f"brep{i}" for i in range(2)] + [f"crep{i}" for i in range(2)]
               + [f"xsD{b}{j}" for b in range(2) for j in range(2)]
               + [f"xsP{b}" for b in range(2)]
               + [f"wzxA{b}" for b in range(2)] + [f"wzxB{b}" for b in range(2)]
               + [f"dg{b}" for b in range(2)]
               + ["sp1", "sp20", "sp21", "sp30", "sp31", "yfin", "ptmp"]
               + [f"sa{o}" for o in range(KC)] + ["x1b"]
               + ["vbf", "sqbf", "kr", "zt", "x1e", "x1e0"]
               + [f"rbf{o}" for o in range(KC)] + [f"k2{o}" for o in range(KC)]
               + [f"xr{k}" for k in range(KC)] + [f"xk{k}" for k in range(KC)]
               + ["xfull"] + [f"wst{i}" for i in range(2)])

        def overlay_barrier():
            need = {}
            for nm in OVL:
                st = res.get(nm)
                if not st:
                    continue
                evs = ([st["w"]] if st["w"] else []) + list(st["r"])
                for s, v in evs:
                    need[s] = max(need.get(s, 0), v)
            evs = sorted(need.items())
            for nm in OVL:
                res[nm] = {"w": None, "r": list(evs)}

        emit("sync", lambda e: e.dma_start(out=V(o_A, DC * NS, F32), in_=dAlog),
             writes=["A"])
        for dr_, off, cols, nm in [
            (dD, o_D, DC, "D"), (dbdt, o_bdt, DC, "bdt"),
            (dcw, o_cw, DC * 4, "cw"), (dcb, o_cb, DC, "cb"),
            (dlng, o_lng, KC, "lng"), (dlnb, o_lnb, KC, "lnb"),
            (dmur, o_mur, KC, "mur"), (dmuk, o_muk, KC, "muk"),
        ]:
            emit("sync", lambda e, dr_=dr_, off=off, cols=cols: e.dma_start(
                out=V(off, cols, F32), in_=dr_), writes=[nm])
        emit("scalar", lambda e: e.activation(
            out=V(o_A, DC * NS, F32), in_=V(o_A, DC * NS, F32), func=AF.Exp),
            reads=["A"], writes=["A"])
        emit("vector", lambda e: e.tensor_scalar_mul(
            V(o_A, DC * NS, F32), V(o_A, DC * NS, F32), -1.0),
            reads=["A"], writes=["A"])
        emit("vector", lambda e: e.memset(V(o_onesb, 1, BF16), 1.0),
             writes=["onesb"])
        emit("vector", lambda e: e.memset(V(o_ceps, 1, F32), EPS), writes=["ceps"])
        emit("vector", lambda e: e.memset(V(o_cmh, 1, F32), -0.5), writes=["cmh"])
        emit("vector", lambda e: e.memset(V(o_cm1, 1, F32), -1.0), writes=["cm1"])
        emit("vector", lambda e: e.memset(V(o_c1, 1, F32), 1.0), writes=["c1"])
        emit("vector", lambda e: e.memset(V(o_bcl, 128, F32, 0, 1), 1.0),
             writes=["bcl"])
        emit("vector", lambda e: e.memset(V(o_x1c, KC, F32), 0.0),
             writes=["x1c"])
        emit("vector", lambda e: e.tensor_copy(
            out=V(o_cwb, DC * 4, BF16), in_=V(o_cw, DC * 4, F32)),
            reads=["cw"], writes=["cwb"])

        emit("sync", lambda e: e.dma_start(
            out=V(o_stg2[0], KC * 96, F32)
            .rearrange("p (k c) -> p k c", c=96)[:, :, 0:64],
            in_=dWin[0:DM, 4128:4192].rearrange("(k p) c -> p k c", p=128)),
            writes=["stg0"])
        emit("sync", lambda e: e.dma_start(
            out=V(o_stg2[0], KC * 96, F32)
            .rearrange("p (k c) -> p k c", c=96)[:, :, 64:96],
            in_=dWin[0:DM, 4096:4128].rearrange("(k p) c -> p k c", p=128)),
            writes=["stg0"])
        emit("scalar", lambda e: e.activation(
            out=V(o_wbcd, KC * 96, BF16), in_=V(o_stg2[0], KC * 96, F32),
            func=AF.Copy),
            reads=["stg0"], writes=["wbcd"])
        emit("sync", lambda e: e.dma_start(
            out=V(o_stg2[1], DI, F32, 0, 64), in_=dWdt),
            writes=["stg1"])
        emit("vector", lambda e: e.tensor_copy(
            out=V(o_wdt, DI, BF16, 0, 64), in_=V(o_stg2[1], DI, F32, 0, 64)),
            reads=["stg1"], writes=["wdt"])
        for r in range(DM // 128):
            p = r % 2
            emit("sync", lambda e, r=r, p=p: e.dma_start(
                out=V(o_stg2[p], TH, F32),
                in_=dx[r * 128:(r + 1) * 128, 0:TH]),
                writes=[f"stg{p}"])
            emit("vector", lambda e, r=r, p=p: e.tensor_copy(
                out=V(o_xbh + r * TH * 2, TH, BF16),
                in_=V(o_stg2[p], TH, F32)),
                reads=[f"stg{p}"], writes=["x_bh"])
        for ci, (c0_, tags_) in enumerate([
                (0, ["sWz0", "sWz1", "sWz2", "sWz3"]),
                (2048, ["sWx0", "sWx1", "sWx2", "sWx3"])]):
            p = ci % 2
            emit("scalar_dma", lambda e, c0=c0_, p=p: e.dma_start(
                out=V(o_stg2[p], KC * 512, F32)
                .rearrange("p (k c) -> p k c", c=512),
                in_=dWin[0:DM, c0:c0 + 512]
                .rearrange("(k p) c -> p k c", p=128)),
                writes=[f"stg{p}"])
            emit("vector", lambda e, p=p: e.tensor_copy(
                out=V(o_stgb2[p], KC * 512, BF16),
                in_=V(o_stg2[p], KC * 512, F32)),
                reads=[f"stg{p}"], writes=[f"stgb{p}"])
            emit("gpsimd_dma", lambda e, c0=c0_, p=p: e.dma_start(
                out=sWin[0:DM, c0:c0 + 512]
                .rearrange("(k p) c -> p k c", p=128),
                in_=V(o_stgb2[p], KC * 512, BF16)
                .rearrange("p (k c) -> p k c", c=512)),
                reads=[f"stgb{p}"], writes=tags_)

        late_jobs = []
        lc_ctr = [0]

        def late_stg3(load_fn, store_fn, tags):
            def job(load_fn=load_fn, store_fn=store_fn, tags=tags):
                p = lc_ctr[0] % 2
                lc_ctr[0] += 1
                emit("scalar_dma", lambda e, p=p: load_fn(e, p),
                     writes=[f"stg3_{p}"])
                emit("scalar", lambda e, p=p: e.activation(
                    out=V(o_stgb3[p], 512, BF16),
                    in_=V(o_stg3[p], 512, F32), func=AF.Copy),
                    reads=[f"stg3_{p}"], writes=[f"stgb3_{p}"])
                emit("gpsimd_dma", lambda e, p=p: store_fn(e, p),
                     reads=[f"stgb3_{p}"], writes=tags)
            late_jobs.append(job)

        for dc_ in range(GRP, DC):
            for base, tname in ((0, "sWz"), (2048, "sWx")):
                for cb in range(2):
                    c0_ = base + dc_ * 128 + cb * 64
                    late_stg3(
                        lambda e, p, c0=c0_: e.dma_start(
                            out=V(o_stg3[p], 512, F32)
                            .rearrange("p (k c) -> p k c", c=64),
                            in_=dWin[0:DM, c0:c0 + 64]
                            .rearrange("(k p) c -> p k c", p=128)),
                        lambda e, p, c0=c0_: e.dma_start(
                            out=sWin[0:DM, c0:c0 + 64]
                            .rearrange("(k p) c -> p k c", p=128),
                            in_=V(o_stgb3[p], 512, BF16)
                            .rearrange("p (k c) -> p k c", c=64)),
                        [f"{tname}{dc_}"] if cb == 1 else [])
        for r in range(DM // 128):
            for cb in range(2):
                c0_ = TH + cb * 512
                late_stg3(
                    lambda e, p, r=r, c0=c0_: e.dma_start(
                        out=V(o_stg3[p], 512, F32),
                        in_=dx[r * 128:(r + 1) * 128, c0:c0 + 512]),
                    lambda e, p, r=r, c0=c0_: e.dma_start(
                        out=sXbf[r * 128:(r + 1) * 128, c0:c0 + 512],
                        in_=V(o_stgb3[p], 512, BF16)),
                    ["sXbf1"] if (r == DM // 128 - 1 and cb == 1) else [])

        def late_cast(dsrc, ddst, rows, tag):
            nchunks = rows // 128
            for r in range(nchunks):
                for cb in range(2):
                    late_stg3(
                        lambda e, p, r=r, cb=cb, dsrc=dsrc: e.dma_start(
                            out=V(o_stg3[p], 512, F32),
                            in_=dsrc[r * 128:(r + 1) * 128,
                                     cb * 512:(cb + 1) * 512]),
                        lambda e, p, r=r, cb=cb, ddst=ddst: e.dma_start(
                            out=ddst[r * 128:(r + 1) * 128,
                                     cb * 512:(cb + 1) * 512],
                            in_=V(o_stgb3[p], 512, BF16)),
                        [tag] if (r == nchunks - 1 and cb == 1) else [])

        late_cast(dWout, sWout, DI, "sWout")
        late_cast(dWr, sWr, DM, "sWr")
        late_cast(dWk, sWk, DM, "sWk")
        late_cast(dWv, sWv, DM, "sWv")

        emit("vector", lambda e: e.memset(V(o_stgb2[0], 4096, BF16), 0.0),
             writes=["stgb0"])
        for i in range(2):
            emit("sync", lambda e, i=i: e.dma_start(
                out=sDiag[i * 4096:(i + 1) * 4096, :]
                .rearrange("(p a) c -> p (a c)", p=128),
                in_=V(o_stgb2[0], 4096, BF16)),
                reads=["stgb0"], writes=[f"diagz{i}"])
        sDiag_f = sDiag.rearrange("a b -> (a b)")
        for idx in range(DC * 4):
            dst = sDiag_f[idx * 16384:(idx + 1) * 16384][0:16384:129]
            emit("sync", lambda e, dst=dst, idx=idx: e.dma_start(
                out=dst, in_=V(o_cwb, DC * 4, BF16)[:, idx:idx + 1],
                allow_slow_non_contiguous=True),
                reads=["cwb", "diagz0", "diagz1"],
                writes=["diag"] if idx == DC * 4 - 1 else [])

        PS = {"a": ps_a, "b": ps_b, "c": ps_c, "d": ps_d, "e": ps_e,
              "f": ps_f, "g": ps_g, "h": ps_h}

        def mm(e, psm, lhsT, rhs, start, stop):
            return e.matmul(psm, lhsT, rhs, start=start, stop=stop)

        for h in range(2):
            hb = h * TH
            overlay_barrier()

            def emit_s1(h_, hb_):
                if h_ == 1:
                    emit("sync", lambda e, hb=hb_: e.dma_start(
                        out=V(o_xbh, KC * TH, BF16)
                        .rearrange("p (k t) -> p k t", t=TH),
                        in_=sXbf[0:DM, hb:hb + TH]
                        .rearrange("(k p) t -> p k t", p=128)),
                        reads=["sXbf1"], writes=["x_bh"])
                for j in range(2):
                    pnm = "ps_" + ("a" if j == 0 else "b")
                    psm = PS[pnm[3]]

                    def mm_bcd(e, j=j, psm=psm):
                        last = None
                        for kc in range(KC):
                            last = mm(e, psm[0:96, :],
                                      V(o_wbcd + kc * 96 * 2, 96, BF16),
                                      V(o_xbh + kc * TH * 2 + (j * NJ) * 2,
                                        NJ, BF16),
                                      kc == 0, kc == KC - 1)
                        return last
                    emit("tensor", mm_bcd,
                         reads=["wbcd", "x_bh"],
                         writes=[pnm])
                    emit("scalar", lambda e, j=j, psm=psm: e.activation(
                        out=V(o_dtin + j * NJ * 2, NJ, BF16, 0, 64),
                        in_=psm[0:64, :], func=AF.Copy),
                        reads=[pnm], writes=[f"dtin{j}"])
                    emit("scalar", lambda e, j=j, psm=psm: e.activation(
                        out=V(o_bcsb + j * NJ * 2, NJ, F16, 64, 96),
                        in_=psm[64:96, :], func=AF.Copy),
                        reads=[pnm], writes=[f"bcsb{j}"])
                emit("sync", lambda e: e.dma_start(
                    out=sBC, in_=V(o_bcsb, TH, F16, 64, 96)),
                    reads=["bcsb0", "bcsb1"], writes=["sBC"])
            if h == 0:
                emit_s1(0, 0)

            unit = [0]
            ev_scan, ev_ym = {}, {}
            NGRP = DC // GRP
            usched = [((g_ % 2) * GRP + i0_, g_ * GRP + i0_, n_)
                      for g_ in range(NGRP) for n_ in range(NS)
                      for i0_ in range(GRP)]
            nxe = [0]

            def drain_exps(upto):
                while nxe[0] <= upto and nxe[0] < len(usched):
                    ue = nxe[0]
                    nxe[0] += 1
                    ie, dce, ne = usched[ue]
                    emit("scalar", lambda e, i=ie, ub=ue % 6, dc=dce,
                         n=ne: e.activation(
                             out=V(o_da[ub], TH, F16),
                             in_=V(o_dt[i], TH, F16), func=AF.Exp,
                             scale=V(o_A, DC * NS, F32)
                             [:, dc * NS + n:dc * NS + n + 1]),
                         reads=[f"dt{i}_0", f"dt{i}_1", "A"],
                         writes=[f"da{ue % 6}"])

            def emit_prod(g, only_i0=None, part=None, h=h, hb=hb):
                dcs = [g * GRP + i for i in range(GRP)]
                gp = (g % 2) * GRP
                for i0, dc in enumerate(dcs):
                    if part == 2 or (only_i0 is not None and i0 != only_i0):
                        continue
                    i = gp + i0
                    bf = i0 % 2
                    emit("sync", lambda e, dc=dc, bf=bf: e.dma_start(
                        out=V(o_wzx[bf], KC * 128, BF16)
                        .rearrange("p (k c) -> p k c", c=128),
                        in_=sWin[0:DM, dc * 128:(dc + 1) * 128]
                        .rearrange("(k p) c -> p k c", p=128)),
                        reads=[f"sWz{dc}"], writes=[f"wzxA{bf}"])
                    emit("sync", lambda e, dc=dc, bf=bf: e.dma_start(
                        out=V(o_wzx[bf] + KC * 128 * 2, KC * 128, BF16)
                        .rearrange("p (k c) -> p k c", c=128),
                        in_=sWin[0:DM, 2048 + dc * 128:2048 + (dc + 1) * 128]
                        .rearrange("(k p) c -> p k c", p=128)),
                        reads=[f"sWx{dc}"], writes=[f"wzxB{bf}"])
                    emit("sync", lambda e, dc=dc, bf=bf: e.dma_start(
                        out=V(o_dg[bf], 4 * 128, BF16)
                        .rearrange("p (f c) -> p f c", c=128),
                        in_=sDiag[dc * 512:(dc + 1) * 512, :]
                        .rearrange("(f p) c -> p f c", p=128)),
                        reads=["diag"], writes=[f"dg{bf}"])
                    for j in range(2):
                        pnm = "ps_" + ("a" if j == 0 else "b")
                        psm = PS[pnm[3]]

                        def mm_xs(e, bf=bf, j=j, psm=psm):
                            last = None
                            for kc in range(KC):
                                last = mm(e, psm[:],
                                          V(o_wzx[bf] + kc * 128 * 2, 128, BF16),
                                          V(o_xbh + kc * TH * 2 + (j * NJ) * 2,
                                            NJ, BF16),
                                          kc == 0, kc == KC - 1)
                            return last
                        emit("tensor", mm_xs, reads=[f"wzxA{bf}", "x_bh"],
                             writes=[pnm])
                        emit("scalar", lambda e, bf=bf, j=j, psm=psm: e.activation(
                            out=V(o_xs[bf] + (3 + j * NJ) * 2, NJ, BF16),
                            in_=psm[:], func=AF.Copy),
                            reads=[pnm], writes=[f"xsD{bf}{j}"])
                    if h == 0:
                        emit("vector", lambda e, bf=bf: e.memset(
                            V(o_xs[bf], 3, BF16), 0.0), writes=[f"xsP{bf}"])
                    else:
                        emit("scalar", lambda e, bf=bf, dc=dc: e.activation(
                            out=V(o_xs[bf], 3, BF16),
                            in_=V(o_xsc + dc * 4 * 2, 3, BF16), func=AF.Copy),
                            reads=[f"xsc{dc}"], writes=[f"xsP{bf}"])
                    if h == 0:
                        emit("scalar", lambda e, bf=bf, dc=dc: e.activation(
                            out=V(o_xsc + dc * 4 * 2, 3, BF16),
                            in_=V(o_xs[bf] + (3 + TH - 3) * 2, 3, BF16),
                            func=AF.Copy),
                            reads=[f"xsD{bf}1"], writes=[f"xsc{dc}"])
                    for j in range(2):
                        pnm = "ps_" + ("c" if j == 0 else "d")
                        psm = PS[pnm[3]]

                        def mm_cv(e, bf=bf, j=j, psm=psm):
                            last = None
                            for tp in range(4):
                                last = mm(e, psm[:],
                                          V(o_dg[bf] + tp * 128 * 2, 128, BF16),
                                          V(o_xs[bf] + (j * NJ + tp) * 2, NJ, BF16),
                                          tp == 0, tp == 3)
                            return last
                        emit("tensor", mm_cv,
                             reads=[f"dg{bf}", f"xsD{bf}0", f"xsD{bf}1",
                                    f"xsP{bf}"],
                             writes=[pnm])
                        emit("scalar", lambda e, i=i, j=j, psm=psm, dc=dc:
                             e.activation(
                                 out=V(o_u[i] + j * NJ * 2, NJ, F16),
                                 in_=psm[:], func=AF.Silu,
                                 bias=V(o_cb, DC, F32)[:, dc:dc + 1]),
                             reads=[pnm, "cb"], writes=[f"u{i}_{j}"])
                    for j in range(2):
                        pnm = "ps_" + ("e" if j == 0 else "f")
                        psm = PS[pnm[3]]

                        def mm_z(e, bf=bf, j=j, psm=psm):
                            last = None
                            for kc in range(KC):
                                last = mm(e, psm[:],
                                          V(o_wzx[bf] + (KC + kc) * 128 * 2, 128,
                                            BF16),
                                          V(o_xbh + kc * TH * 2 + (j * NJ) * 2,
                                            NJ, BF16),
                                          kc == 0, kc == KC - 1)
                            return last
                        emit("tensor", mm_z, reads=[f"wzxB{bf}", "x_bh"],
                             writes=[pnm])
                        emit("scalar", lambda e, i=i, j=j, psm=psm: e.activation(
                            out=V(o_sz[i] + j * NJ * 2, NJ, F16),
                            in_=psm[:], func=AF.Silu),
                            reads=[pnm], writes=[f"sz{i}_{j}"])
                for i0, dc in enumerate(dcs):
                    if part == 1:
                        continue
                    i = gp + i0
                    for j in range(2):
                        pnm = "ps_" + ("g" if (i0 % 2) == 0 else "h")
                        psm = PS[pnm[3]]
                        sj = i0 % 2
                        emit("tensor", lambda e, dc=dc, j=j, psm=psm: mm(
                            e, psm[0:128, j * 0:512] if False else psm[:],
                            V(o_wdt + dc * 128 * 2, 128, BF16, 0, 64),
                            V(o_dtin + j * NJ * 2, NJ, BF16, 0, 64),
                            True, True),
                            reads=["wdt", f"dtin{j}"], writes=[pnm])
                        emit("scalar", lambda e, psm=psm, dc=dc: e.activation(
                            out=V(o_sp1, NJ, F32), in_=psm[:], func=AF.Abs,
                            bias=V(o_bdt, DC, F32)[:, dc:dc + 1]),
                            reads=[pnm, "bdt"], writes=["sp1"])
                        emit("scalar", lambda e, sj=sj: e.activation(
                            out=V(o_sp2[sj], NJ, F16), in_=V(o_sp1, NJ, F32),
                            func=AF.Exp, scale=V(o_cm1, 1, F32)),
                            reads=["sp1", "cm1"], writes=[f"sp2{sj}"])
                        emit("scalar", lambda e, sj=sj: e.activation(
                            out=V(o_sp3[sj], NJ, F16), in_=V(o_sp2[sj], NJ, F16),
                            func=AF.Ln, bias=V(o_c1, 1, F32)),
                            reads=[f"sp2{sj}", "c1"], writes=[f"sp3{sj}"])
                        emit("scalar", lambda e, psm=psm, dc=dc, sj=sj: e.activation(
                            out=V(o_sp2[sj], NJ, F16), in_=psm[:], func=AF.Relu,
                            bias=V(o_bdt, DC, F32)[:, dc:dc + 1]),
                            reads=[pnm, "bdt"], writes=[f"sp2{sj}"])
                        emit("vector", lambda e, i=i, j=j, sj=sj: e.tensor_tensor(
                            V(o_dt[i] + j * NJ * 2, NJ, F16),
                            V(o_sp3[sj], NJ, F16), V(o_sp2[sj], NJ, F16), ALU.add),
                            reads=[f"sp2{sj}", f"sp3{sj}"], writes=[f"dt{i}_{j}"])
                    emit("vector", lambda e, i=i: e.tensor_tensor(
                        V(o_dtu[i], TH, F16), V(o_dt[i], TH, F16),
                        V(o_u[i], TH, F16), ALU.mult),
                        reads=[f"dt{i}_0", f"dt{i}_1", f"u{i}_0", f"u{i}_1"],
                        writes=[f"dtu{i}"])
                    tap(f"xs_h{h}_dc{dc}", V(o_xs[i0 % 2], TH + 4, BF16),
                        [f"xsD{i0 % 2}0", f"xsD{i0 % 2}1", f"xsP{i0 % 2}"])
                    tap(f"u_h{h}_dc{dc}", V(o_u[i], TH, F16),
                        [f"u{i}_0", f"u{i}_1"])
                    tap(f"sz_h{h}_dc{dc}", V(o_sz[i], TH, F16),
                        [f"sz{i}_0", f"sz{i}_1"])
                    tap(f"dt_h{h}_dc{dc}", V(o_dt[i], TH, F16),
                        [f"dt{i}_0", f"dt{i}_1"])
                    tap(f"dtu_h{h}_dc{dc}", V(o_dtu[i], TH, F16), [f"dtu{i}"])


            emit_prod(0)
            for g in range(NGRP):
                dcs = [g * GRP + i for i in range(GRP)]
                gp = (g % 2) * GRP
                for n in range(NS):
                    pb = n % 2
                    emit("sync", lambda e, pb=pb, n=n: e.dma_start(
                        out=V(o_brep[pb], TH, F16),
                        in_=sBC[n:n + 1, :].broadcast_to([128, TH])),
                        reads=["sBC"], writes=[f"brep{pb}"])
                    emit("sync", lambda e, pb=pb, n=n: e.dma_start(
                        out=V(o_crep[pb], TH, F16),
                        in_=sBC[16 + n:17 + n, :].broadcast_to([128, TH])),
                        reads=["sBC"], writes=[f"crep{pb}"])
                    if h == 0:
                        for _ in range(3):
                            if late_jobs:
                                late_jobs.pop(0)()
                    if n == 10 and g + 1 < NGRP:
                        emit_prod(g + 1, part=1)
                    if n == 13 and g + 1 < NGRP:
                        emit_prod(g + 1, part=2)
                    for i0, dc in enumerate(dcs):
                        i = gp + i0
                        u_ = unit[0]
                        unit[0] += 1
                        drain_exps(u_ + 5)
                        ub = u_ % 6
                        emit("vector", lambda e, i=i, pb=pb:
                             e.tensor_tensor(
                                 V(o_bm[0], TH, F16), V(o_dtu[i], TH, F16),
                                 V(o_brep[pb], TH, F16), ALU.mult),
                             reads=[f"dtu{i}", f"brep{pb}"], writes=["bm0"])

                        hb_ = u_ % 4

                        def f_scan(e, ub=ub, hb_=hb_, dc=dc, n=n, h=h):
                            init = (0.0 if h == 0 else
                                    V(o_carry, DC * NS, F32)
                                    [:, dc * NS + n:dc * NS + n + 1])
                            return e.tensor_tensor_scan(
                                V(o_hh[hb_], TH, F16), V(o_da[ub], TH, F16),
                                V(o_bm[0], TH, F16), init, ALU.mult, ALU.add)
                        ev_scan[u_] = emit(
                            "vector", f_scan,
                            reads=[f"da{ub}", "bm0"] +
                                  (["carry"] if h == 1 else []),
                            writes=[f"hh{hb_}"])
                        tap(f"h_h{h}_dc{dc}_n{n}", V(o_hh[hb_], TH, F16),
                            [f"hh{hb_}"])
                        tap(f"da_h{h}_dc{dc}_n{n}", V(o_da[ub], TH, F16),
                            [f"da{ub}"])
                        tap(f"bm_h{h}_dc{dc}_n{n}", V(o_bm[0], TH, F16),
                            ["bm0"])
                        if h == 0:
                            emit("scalar", lambda e, hb_=hb_, dc=dc, n=n:
                                 e.activation(
                                     out=V(o_carry, DC * NS, F32)
                                     [:, dc * NS + n:dc * NS + n + 1],
                                     in_=V(o_hh[hb_] + (TH - 1) * 2, 1, F16),
                                     func=AF.Copy),
                                 reads=[f"hh{hb_}"], writes=["carry_el"])
                        if n % 2 == 0:
                            emit("vector", lambda e, hb_=hb_, pb=pb, i0=i0:
                                 e.tensor_tensor(
                                     V(o_ymE[i0], TH, F16), V(o_hh[hb_], TH, F16),
                                     V(o_crep[pb], TH, F16), ALU.mult),
                                 reads=[f"hh{hb_}", f"crep{pb}"],
                                 writes=[f"ymE{i0}"])
                        else:
                            emit("vector", lambda e, hb_=hb_, pb=pb:
                                 e.tensor_tensor(
                                     V(o_ymT, TH, F16), V(o_hh[hb_], TH, F16),
                                     V(o_crep[pb], TH, F16), ALU.mult),
                                 reads=[f"hh{hb_}", f"crep{pb}"], writes=["ymT"])
                            emit("vector", lambda e, i0=i0: e.tensor_tensor(
                                V(o_ptmp, TH, F16), V(o_ymE[i0], TH, F16),
                                V(o_ymT, TH, F16), ALU.add),
                                reads=[f"ymE{i0}", "ymT"], writes=["ptmp", "bm0"])
                            if n == 1:
                                emit("vector", lambda e, i0=i0: e.tensor_copy(
                                    out=V(o_acc[i0], TH, F16),
                                    in_=V(o_ptmp, TH, F16)),
                                    reads=["ptmp", "bm0"], writes=[f"accT{i0}"])
                            else:
                                emit("vector", lambda e, i0=i0: e.tensor_tensor(
                                    V(o_acc[i0], TH, F16), V(o_acc[i0], TH, F16),
                                    V(o_ptmp, TH, F16), ALU.add),
                                    reads=["ptmp", "bm0", f"accT{i0}"],
                                    writes=[f"accT{i0}"])
                for i0, dc in enumerate(dcs):
                    i = gp + i0
                    emit("vector", lambda e, i=i, i0=i0, dc=dc:
                         e.scalar_tensor_tensor(
                             V(o_yfin, TH, F16), V(o_u[i], TH, F16),
                             V(o_D, DC, F32)[:, dc:dc + 1], V(o_acc[i0], TH, F16),
                             ALU.mult, ALU.add),
                         reads=[f"accT{i0}", f"u{i}_0", f"u{i}_1", "D"],
                         writes=["yfin", "bm0"])
                    emit("vector", lambda e, i=i, dc=dc: e.tensor_tensor(
                        V(o_yg + dc * TH * 2, TH, BF16), V(o_yfin, TH, F16),
                        V(o_sz[i], TH, F16), ALU.mult),
                        reads=["yfin", "bm0", f"sz{i}_0", f"sz{i}_1"],
                        writes=[f"yg{dc}"])
                    tap(f"yg_h{h}_dc{dc}", V(o_yg + dc * TH * 2, TH, BF16),
                        [f"yg{dc}"])

            while late_jobs:
                late_jobs.pop(0)()
            if h == 0:
                emit_s1(1, TH)
            overlay_barrier()
            for j in range(2):
                jb = j * NJ
                emit("sync", lambda e, hb=hb, jb=jb: e.dma_start(
                    out=V(o_xfull, KC * NJ, F32)
                    .rearrange("p (o t) -> p o t", t=NJ),
                    in_=dx[0:DM, hb + jb:hb + jb + NJ]
                    .rearrange("(o p) t -> p o t", p=128)),
                    writes=["xfull"])
                ev_attn = {}
                for oc in range(KC):
                    wb = oc % 2
                    emit("sync", lambda e, oc=oc, wb=wb: e.dma_start(
                        out=V(o_wst[wb], 16 * 128, BF16)
                        .rearrange("p (k c) -> p k c", c=128),
                        in_=sWout[0:DI, oc * 128:(oc + 1) * 128]
                        .rearrange("(k p) c -> p k c", p=128)),
                        reads=["sWout"], writes=[f"wst{wb}"])
                    pnm = "ps_" + ("a" if oc % 2 == 0 else "b")
                    psm = PS[pnm[3]]

                    def mm_wo(e, wb=wb, psm=psm, jb=jb):
                        last = None
                        for kc in range(DC):
                            last = mm(e, psm[:],
                                      V(o_wst[wb] + kc * 128 * 2, 128, BF16),
                                      V(o_yg + kc * TH * 2 + jb * 2, NJ, BF16),
                                      kc == 0, kc == DC - 1)
                        return last
                    emit("tensor", mm_wo,
                         reads=[f"wst{wb}"] + [f"yg{d}" for d in range(DC)],
                         writes=[pnm])
                    emit("vector", lambda e, oc=oc, psm=psm:
                         e.scalar_tensor_tensor(
                             V(o_sa_ + oc * NJ * 4, NJ, F32), psm[:], 1.0,
                             V(o_xfull + oc * NJ * 4, NJ, F32),
                             ALU.mult, ALU.add),
                         reads=[pnm, "xfull"], writes=[f"sa{oc}"])

                SA = [f"sa{o}" for o in range(KC)]

                def bview(psm):
                    return psm[:].unsqueeze(1).broadcast_to([128, KC, NJ])

                def batched(off, dt_):
                    return V(off, KC * NJ, dt_).rearrange(
                        "p (o t) -> p o t", t=NJ)

                def ln_stats(tag):
                    emit("vector", lambda e: e.tensor_copy(
                        out=V(o_vbf, KC * NJ, BF16),
                        in_=V(o_sa_, KC * NJ, F32)),
                        reads=SA, writes=["vbf"])
                    emit("scalar", lambda e: e.activation(
                        out=V(o_sqbf, KC * NJ, BF16),
                        in_=V(o_sa_, KC * NJ, F32), func=AF.Square),
                        reads=SA, writes=["sqbf"])

                    def mm_sums(e):
                        last = None
                        for kc in range(KC):
                            mm(e, ps_c[0:1, :], V(o_onesb, 1, BF16),
                               V(o_vbf + kc * NJ * 2, NJ, BF16),
                               kc == 0, kc == KC - 1)
                        for kc in range(KC):
                            last = mm(e, ps_c[32:33, :], V(o_onesb, 1, BF16),
                                      V(o_sqbf + kc * NJ * 2, NJ, BF16),
                                      kc == 0, kc == KC - 1)
                        return last
                    emit("tensor", mm_sums, reads=["vbf", "sqbf", "onesb"],
                         writes=["ps_c"])
                    emit("vector", lambda e: e.tensor_scalar_mul(
                        R1(o_row1), ps_c[0:1, :], 1.0 / DM),
                        reads=["ps_c"], writes=["rowa"])
                    emit("vector", lambda e: e.tensor_tensor(
                        R1(o_row1, 32), R1(o_row1), R1(o_row1), ALU.mult),
                        reads=["rowa"], writes=["rowd"])
                    emit("vector", lambda e: e.scalar_tensor_tensor(
                        R1(o_row1, 64), ps_c[32:33, :], 1.0 / DM,
                        R1(o_row1, 32), ALU.mult, ALU.subtract),
                        reads=["ps_c", "rowd"], writes=["rowe"])
                    emit("scalar", lambda e: e.activation(
                        out=R1(o_row1, 32), in_=R1(o_row1, 64), func=AF.Ln,
                        bias=V(o_ceps, 1, F32, 0, 1)),
                        reads=["rowe", "ceps"], writes=["rowd"])
                    emit("scalar", lambda e: e.activation(
                        out=R1(o_row2), in_=R1(o_row1, 32), func=AF.Exp,
                        scale=V(o_cmh, 1, F32, 0, 1)),
                        reads=["rowd", "cmh"], writes=["rowb"])
                    emit("vector", lambda e: e.tensor_tensor(
                        R1(o_row3), R1(o_row1), R1(o_row2), ALU.mult),
                        reads=["rowa", "rowb"], writes=["rowc"])

                    def mm_bc(e):
                        mm(e, ps_d[:], V(o_bcl, 128, F32, 0, 1), R1(o_row2),
                           True, True)
                        return mm(e, ps_e[:], V(o_bcl, 128, F32, 0, 1),
                                  R1(o_row3), True, True)
                    emit("tensor", mm_bc, reads=["rowb", "rowc", "bcl"],
                         writes=["ps_d", "ps_e"])

                ln_stats("ln0")
                emit("vector", lambda e: e.tensor_tensor(
                    batched(o_x1, F32), batched(o_sa_, F32), bview(ps_d),
                    ALU.mult),
                    reads=SA + ["ps_d"], writes=["x1b"])
                emit("vector", lambda e: e.tensor_tensor(
                    batched(o_x1, F32), batched(o_x1, F32), bview(ps_e),
                    ALU.subtract),
                    reads=["x1b", "ps_e"], writes=["x1b"])
                for oc in range(KC):
                    emit("vector", lambda e, oc=oc: e.scalar_tensor_tensor(
                        V(o_sa_ + oc * NJ * 4, NJ, F32),
                        V(o_x1 + oc * NJ * 4, NJ, F32),
                        V(o_lng, KC, F32)[:, oc:oc + 1],
                        V(o_xfull + oc * NJ * 4, NJ, F32),
                        ALU.mult, ALU.add),
                        reads=["x1b", "lng", "xfull"], writes=[f"sa{oc}"])
                    emit("vector", lambda e, oc=oc: e.tensor_scalar_add(
                        V(o_sa_ + oc * NJ * 4, NJ, F32),
                        V(o_sa_ + oc * NJ * 4, NJ, F32),
                        V(o_lnb, KC, F32)[:, oc:oc + 1]),
                        reads=["lnb", f"sa{oc}"], writes=[f"sa{oc}"])
                for oc in range(KC):
                    tap(f"attnr1_h{h}_j{j}_oc{oc}",
                        V(o_sa_ + oc * NJ * 4, NJ, F32), [f"sa{oc}"])
                ln_stats("ln1")
                emit("vector", lambda e: e.tensor_tensor(
                    batched(o_x1, F32), batched(o_sa_, F32), bview(ps_d),
                    ALU.mult),
                    reads=SA + ["ps_d"], writes=["x1b"])
                emit("vector", lambda e: e.tensor_tensor(
                    batched(o_x1, F32), batched(o_x1, F32), bview(ps_e),
                    ALU.subtract),
                    reads=["x1b", "ps_e"], writes=["x1b"])
                for oc in range(KC):
                    tap(f"x1_h{h}_j{j}_oc{oc}",
                        V(o_x1 + oc * NJ * 4, NJ, F32), ["x1b"])

                for kc in range(KC):
                    emit("vector", lambda e, kc=kc: e.tensor_copy(
                        out=V(o_x1e + 4, NJ, F32),
                        in_=V(o_x1 + kc * NJ * 4, NJ, F32)),
                        reads=["x1b"], writes=["x1e"])
                    emit("vector", lambda e, kc=kc: e.tensor_copy(
                        out=V(o_x1e, 1, F32), in_=V(o_x1c + kc * 4, 1, F32)),
                        reads=["x1c"], writes=["x1e0"])
                    emit("vector", lambda e, kc=kc: e.tensor_copy(
                        out=V(o_x1c + kc * 4, 1, F32),
                        in_=V(o_x1 + kc * NJ * 4 + (NJ - 1) * 4, 1, F32)),
                        reads=["x1b"], writes=["x1c"])
                    emit("vector", lambda e: e.tensor_tensor(
                        V(o_zt, NJ, F32), V(o_x1e + 4, NJ, F32),
                        V(o_x1e, NJ, F32), ALU.subtract),
                        reads=["x1e", "x1e0"], writes=["zt"])
                    emit("vector", lambda e, kc=kc: e.scalar_tensor_tensor(
                        V(o_xr + kc * NJ * 2, NJ, BF16), V(o_zt, NJ, F32),
                        V(o_mur, KC, F32)[:, kc:kc + 1], V(o_x1e, NJ, F32),
                        ALU.mult, ALU.add),
                        reads=["zt", "mur", "x1e", "x1e0"], writes=[f"xr{kc}"])
                    emit("vector", lambda e, kc=kc: e.scalar_tensor_tensor(
                        V(o_xk + kc * NJ * 2, NJ, BF16), V(o_zt, NJ, F32),
                        V(o_muk, KC, F32)[:, kc:kc + 1], V(o_x1e, NJ, F32),
                        ALU.mult, ALU.add),
                        reads=["zt", "muk", "x1e", "x1e0"], writes=[f"xk{kc}"])

                for oc in range(KC):
                    wb = (oc // 2) % 2
                    if oc % 2 == 0:
                        emit("sync", lambda e, oc=oc, wb=wb: e.dma_start(
                            out=V(o_wst[wb], 16 * 128, BF16)
                            .rearrange("p (k m) -> p k m", m=256),
                            in_=sWr[0:DM, oc * 128:(oc + 2) * 128]
                            .rearrange("(k p) m -> p k m", p=128)),
                            reads=["sWr"], writes=[f"wst{wb}"])

                    pr = [ps_f, ps_d][oc % 2]
                    prn = ["ps_f", "ps_d"][oc % 2]

                    ho = (oc % 2) * 128 * 2

                    def mm_r(e, wb=wb, pr=pr, ho=ho):
                        last = None
                        for kc in range(KC):
                            last = mm(e, pr[:],
                                      V(o_wst[wb] + ho + kc * 256 * 2, 128, BF16),
                                      V(o_xr + kc * NJ * 2, NJ, BF16),
                                      kc == 0, kc == KC - 1)
                        return last
                    emit("tensor", mm_r,
                         reads=[f"wst{wb}"] + [f"xr{k}" for k in range(KC)],
                         writes=[prn])
                    emit("scalar", lambda e, oc=oc, pr=pr: e.activation(
                        out=V(o_rbf + oc * NJ * 2, NJ, BF16), in_=pr[:],
                        func=AF.Sigmoid),
                        reads=[prn], writes=[f"rbf{oc}"])
                for oc in range(KC):
                    wb = (oc // 2) % 2
                    if oc % 2 == 0:
                        emit("sync", lambda e, oc=oc, wb=wb: e.dma_start(
                            out=V(o_wst[wb], 16 * 128, BF16)
                            .rearrange("p (k m) -> p k m", m=256),
                            in_=sWk[0:DM, oc * 128:(oc + 2) * 128]
                            .rearrange("(k p) m -> p k m", p=128)),
                            reads=["sWk"], writes=[f"wst{wb}"])

                    pk = [ps_g, ps_e][oc % 2]
                    pkn = ["ps_g", "ps_e"][oc % 2]

                    ho = (oc % 2) * 128 * 2

                    def mm_k(e, wb=wb, pk=pk, ho=ho):
                        last = None
                        for kc in range(KC):
                            last = mm(e, pk[:],
                                      V(o_wst[wb] + ho + kc * 256 * 2, 128, BF16),
                                      V(o_xk + kc * NJ * 2, NJ, BF16),
                                      kc == 0, kc == KC - 1)
                        return last
                    emit("tensor", mm_k,
                         reads=[f"wst{wb}"] + [f"xk{k}" for k in range(KC)],
                         writes=[pkn])
                    emit("scalar", lambda e, pk=pk: e.activation(
                        out=V(o_kr, NJ, BF16), in_=pk[:], func=AF.Relu),
                        reads=[pkn], writes=["kr"])
                    emit("vector", lambda e, oc=oc: e.tensor_tensor(
                        V(o_k2bf + oc * NJ * 2, NJ, BF16), V(o_kr, NJ, BF16),
                        V(o_kr, NJ, BF16), ALU.mult),
                        reads=["kr"], writes=[f"k2{oc}"])
                for oc in range(KC):
                    wb = (oc // 2) % 2
                    if oc % 2 == 0:
                        emit("sync", lambda e, oc=oc, wb=wb: e.dma_start(
                            out=V(o_wst[wb], 16 * 128, BF16)
                            .rearrange("p (k m) -> p k m", m=256),
                            in_=sWv[0:DM, oc * 128:(oc + 2) * 128]
                            .rearrange("(k p) m -> p k m", p=128)),
                            reads=["sWv"], writes=[f"wst{wb}"])

                    pv = [ps_h, ps_g][oc % 2]
                    pvn = ["ps_h", "ps_g"][oc % 2]

                    ho = (oc % 2) * 128 * 2

                    def mm_v(e, wb=wb, pv=pv, ho=ho):
                        last = None
                        for kc in range(KC):
                            last = mm(e, pv[:],
                                      V(o_wst[wb] + ho + kc * 256 * 2, 128, BF16),
                                      V(o_k2bf + kc * NJ * 2, NJ, BF16),
                                      kc == 0, kc == KC - 1)
                        return last
                    emit("tensor", mm_v,
                         reads=[f"wst{wb}"] + [f"k2{k}" for k in range(KC)],
                         writes=[pvn])
                    emit("vector", lambda e, oc=oc, pv=pv: e.tensor_tensor(
                        V(o_zt, NJ, F32), V(o_rbf + oc * NJ * 2, NJ, BF16),
                        pv[:], ALU.mult),
                        reads=[pvn, f"rbf{oc}"], writes=["zt"])
                    emit("vector", lambda e, oc=oc: e.tensor_tensor(
                        V(o_sa_ + oc * NJ * 4, NJ, F32),
                        V(o_x1 + oc * NJ * 4, NJ, F32), V(o_zt, NJ, F32),
                        ALU.add),
                        reads=["zt", "x1b"], writes=[f"sa{oc}"])
                ln_stats("ln2")
                emit("vector", lambda e: e.tensor_tensor(
                    batched(o_x1, F32), batched(o_sa_, F32), bview(ps_d),
                    ALU.mult),
                    reads=SA + ["ps_d"], writes=["x1b"])
                emit("vector", lambda e: e.tensor_tensor(
                    batched(o_x1, F32), batched(o_x1, F32), bview(ps_e),
                    ALU.subtract),
                    reads=["x1b", "ps_e"], writes=["x1b"])
                emit("gpsimd_dma", lambda e, hb=hb, jb=jb: e.dma_start(
                    out=dy[0:DM, hb + jb:hb + jb + NJ]
                    .rearrange("(o p) t -> p o t", p=128),
                    in_=batched(o_x1, F32)),
                    reads=["x1b"], writes=[])

        def fin(e):
            for s in ("sd", "sg", "spe", "sa", "sv", "sp", "sc"):
                if CNT[s]:
                    e.wait_ge(SEM[s], CNT[s])
        sched.append(("gpsimd", fin))

        @blk.sync
        def _(eng):
            for e, fn in sched:
                if e == "sync":
                    fn(eng)

        @blk.tensor
        def _(eng):
            for e, fn in sched:
                if e == "tensor":
                    fn(eng)

        @blk.scalar
        def _(eng):
            for e, fn in sched:
                if e == "scalar":
                    fn(eng)

        @blk.vector
        def _(eng):
            for e, fn in sched:
                if e == "vector":
                    fn(eng)

        @blk.gpsimd
        def _(eng):
            for e, fn in sched:
                if e == "gpsimd":
                    fn(eng)

    mybir.codegen_inst_isa_subclasses(nc)
    return nc


def _prep_inputs(inputs):
    f = np.ascontiguousarray
    shared = {
        "w_in_t": f(inputs["L0_attn_W_in"].T),
        "w_dt_t": f(inputs["L0_attn_W_dt"].T),
        "w_out_t": f(inputs["L0_attn_W_out"].T),
        "w_r_t": f(inputs["L0_ffn_W_r"].T),
        "w_k_t": f(inputs["L0_ffn_W_k"].T),
        "w_v_t": f(inputs["L0_ffn_W_v"].T),
        "a_log_t": f(inputs["L0_attn_A_log"].reshape(DC, 128, NS)
                     .transpose(1, 0, 2).reshape(128, DC * NS)),
        "dd_t": f(inputs["L0_attn_D"].reshape(DC, 128).T),
        "bdt_t": f(inputs["L0_attn_b_dt"].reshape(DC, 128).T),
        "convw_t": f(inputs["L0_attn_conv_w"].reshape(DC, 128, 4)
                     .transpose(1, 0, 2).reshape(128, DC * 4)),
        "convb_t": f(inputs["L0_attn_conv_b"].reshape(DC, 128).T),
        "lng_t": f(inputs["L0_attn_ln_g"].reshape(KC, 128).T),
        "lnb_t": f(inputs["L0_attn_ln_b"].reshape(KC, 128).T),
        "mur_t": f(inputs["L0_ffn_mu_r"].reshape(KC, 128).T),
        "muk_t": f(inputs["L0_ffn_mu_k"].reshape(KC, 128).T),
    }
    x = inputs["L0_attn_x"]
    return [dict(shared, x_t=f(x[b].T)) for b in range(8)]


def kernel(**inputs):
    nc = build_program()
    in_maps = _prep_inputs(inputs)
    res = run_bass_kernel_spmd(nc, in_maps, core_ids=list(range(8)))
    return np.stack([np.ascontiguousarray(r["y_t"].T) for r in res.results])


# version 30
# speedup vs baseline: 1.0001x; 1.0001x over previous
import numpy as np

import concourse.bass as bass
import concourse.mybir as mybir
from concourse.bass_utils import run_bass_kernel_spmd

F32 = mybir.dt.float32
F16 = mybir.dt.float16
BF16 = mybir.dt.bfloat16
U8 = mybir.dt.uint8
AF = mybir.ActivationFunctionType
ALU = mybir.AluOpType

DM, DI, NS, DR = 1024, 2048, 16, 64
T, TH, NJ = 2048, 1024, 512
KC, DC, GRP = 8, 16, 4
EPS = 1e-5
TAPS = []
EMIT_LOG = None
ESZ = {F32: 4, F16: 2, BF16: 2, U8: 1}


class Mem:
    def __init__(self, total, start=0):
        self.total, self.cur = total, start

    def alloc(self, name, cols, dt_):
        off = -(-self.cur // 32) * 32
        self.cur = off + cols * ESZ[dt_]
        assert self.cur <= self.total, f"SBUF overflow at {name}: {self.cur}"
        return off


def build_program():
    nc = bass.Bass("TRN2", target_bir_lowering=False)

    dx = nc.dram_tensor("x_t", [DM, T], F32, kind="ExternalInput").ap()
    dWin = nc.dram_tensor("w_in_t", [DM, 4192], F32, kind="ExternalInput").ap()
    dWdt = nc.dram_tensor("w_dt_t", [DR, DI], F32, kind="ExternalInput").ap()
    dWout = nc.dram_tensor("w_out_t", [DI, DM], F32, kind="ExternalInput").ap()
    dWr = nc.dram_tensor("w_r_t", [DM, DM], F32, kind="ExternalInput").ap()
    dWk = nc.dram_tensor("w_k_t", [DM, DM], F32, kind="ExternalInput").ap()
    dWv = nc.dram_tensor("w_v_t", [DM, DM], F32, kind="ExternalInput").ap()
    dAlog = nc.dram_tensor("a_log_t", [128, DC * NS], F32, kind="ExternalInput").ap()
    dD = nc.dram_tensor("dd_t", [128, DC], F32, kind="ExternalInput").ap()
    dbdt = nc.dram_tensor("bdt_t", [128, DC], F32, kind="ExternalInput").ap()
    dcw = nc.dram_tensor("convw_t", [128, DC * 4], F32, kind="ExternalInput").ap()
    dcb = nc.dram_tensor("convb_t", [128, DC], F32, kind="ExternalInput").ap()
    dlng = nc.dram_tensor("lng_t", [128, KC], F32, kind="ExternalInput").ap()
    dlnb = nc.dram_tensor("lnb_t", [128, KC], F32, kind="ExternalInput").ap()
    dmur = nc.dram_tensor("mur_t", [128, KC], F32, kind="ExternalInput").ap()
    dmuk = nc.dram_tensor("muk_t", [128, KC], F32, kind="ExternalInput").ap()
    dy = nc.dram_tensor("y_t", [DM, T], F32, kind="ExternalOutput").ap()

    sWin = nc.dram_tensor("s_win", [DM, 4192], BF16, kind="Internal").ap()
    sWout = nc.dram_tensor("s_wout", [DI, DM], BF16, kind="Internal").ap()
    sWr = nc.dram_tensor("s_wr", [DM, DM], BF16, kind="Internal").ap()
    sWk = nc.dram_tensor("s_wk", [DM, DM], BF16, kind="Internal").ap()
    sWv = nc.dram_tensor("s_wv", [DM, DM], BF16, kind="Internal").ap()
    sXbf = nc.dram_tensor("s_xbf", [DM, T], BF16, kind="Internal").ap()
    sBC = nc.dram_tensor("s_bc", [32, TH], F16, kind="Internal").ap()
    sDiag = nc.dram_tensor("s_diag", [DC * 4 * 128, 128], BF16, kind="Internal").ap()

    TOT = 206 * 1024
    mem = Mem(TOT)
    o_yg = mem.alloc("y_g", DC * TH, BF16)
    o_A = mem.alloc("A", DC * NS, F32)
    o_D = mem.alloc("D", DC, F32)
    o_bdt = mem.alloc("bdt", DC, F32)
    o_cw = mem.alloc("convw", DC * 4, F32)
    o_cwb = mem.alloc("convwb", DC * 4, BF16)
    o_cb = mem.alloc("convb", DC, F32)
    o_lng = mem.alloc("lng", KC, F32)
    o_lnb = mem.alloc("lnb", KC, F32)
    o_mur = mem.alloc("mur", KC, F32)
    o_muk = mem.alloc("muk", KC, F32)
    o_onesb = mem.alloc("onesb", 1, BF16)
    o_bcl = mem.alloc("bcl", 128, F32)
    o_carry = mem.alloc("carry", DC * NS, F32)
    o_wdt = mem.alloc("wdt_bf", DI, BF16)
    o_dtin = mem.alloc("dtin_bf", TH, BF16)
    o_wbcd = mem.alloc("wbcd", KC * 96, BF16)
    o_x1c = mem.alloc("x1_carry", KC, F32)
    o_xsc = mem.alloc("xs_carry", DC * 4, F16)
    o_bcsb = mem.alloc("bc_sb", TH, F16)
    o_row1 = mem.alloc("row_1", NJ, F32)
    o_row2 = mem.alloc("row_2", NJ, F32)
    o_row3 = mem.alloc("row_3", NJ, F32)
    o_ceps = mem.alloc("c_eps", 1, F32)
    o_cmh = mem.alloc("c_mh", 1, F32)
    o_cm1 = mem.alloc("c_m1", 1, F32)
    o_c1 = mem.alloc("c_1", 1, F32)
    PERS = mem.cur

    m2 = Mem(TOT, PERS)
    o_dt = [m2.alloc(f"dt{i}", TH, F16) for i in range(2 * GRP)]
    o_u = [m2.alloc(f"u{i}", TH, F16) for i in range(2 * GRP)]
    o_dtu = [m2.alloc(f"dtu{i}", TH, F16) for i in range(2 * GRP)]
    o_sz = [m2.alloc(f"sz{i}", TH, F16) for i in range(2 * GRP)]
    o_acc = [m2.alloc(f"acc{i}", TH, F16) for i in range(GRP)]
    o_da = [m2.alloc(f"da{i}", TH, F16) for i in range(6)]
    o_bm = [m2.alloc(f"bm{i}", TH, F16) for i in range(1)]
    o_ptmp = o_bm[0]
    o_yfin = o_bm[0]
    o_hh = [m2.alloc(f"h{i}", TH, F16) for i in range(4)]
    o_ymE = [m2.alloc(f"ymE{i}", TH, F16) for i in range(GRP)]
    o_ymT = m2.alloc("ymT", TH, F16)
    o_brep = [m2.alloc(f"brep{i}", TH, F16) for i in range(2)]
    o_crep = [m2.alloc(f"crep{i}", TH, F16) for i in range(2)]
    o_xs = [m2.alloc(f"xs{i}", TH + 4, BF16) for i in range(2)]
    o_wzx = [m2.alloc(f"wzx{i}", 2 * KC * 128, BF16) for i in range(2)]
    o_dg = [m2.alloc(f"dg{i}", 4 * 128, BF16) for i in range(2)]
    o_sp1 = m2.alloc("sp1", NJ, F32)
    o_sp2 = [m2.alloc(f"sp2{i}", NJ, F16) for i in range(2)]
    o_sp3 = [m2.alloc(f"sp3{i}", NJ, F16) for i in range(2)]
    o_xbh = m2.alloc("x_bh", KC * TH, BF16)
    S2_END = m2.cur

    m3 = Mem(TOT, PERS)
    o_sa_ = m3.alloc("streamA", KC * NJ, F32)
    o_x1 = m3.alloc("x1", KC * NJ, F32)
    o_vbf = m3.alloc("vbf", KC * NJ, BF16)
    o_sqbf = m3.alloc("sqbf", KC * NJ, BF16)
    o_rbf = m3.alloc("rbf", KC * NJ, BF16)
    o_k2bf = m3.alloc("k2bf", KC * NJ, BF16)
    o_xr = m3.alloc("xr", KC * NJ, BF16)
    o_xk = m3.alloc("xk", KC * NJ, BF16)
    o_kr = m3.alloc("kr", NJ, BF16)
    o_zt = m3.alloc("ztmp", NJ, F32)
    o_x1e = m3.alloc("x1e", NJ + 1, F32)
    o_xfull = m3.alloc("xfull", KC * NJ, F32)
    o_wst = [m3.alloc(f"wst{i}", 16 * 128, BF16) for i in range(2)]
    S3_END = m3.cur

    mt = Mem(TOT, 0)
    mt.cur = 0
    m0 = Mem(TOT, PERS)
    o_stg2 = [m0.alloc(f"stg{i}", 4192, F32) for i in range(2)]
    o_stgb2 = [m0.alloc(f"stgb{i}", 4192, BF16) for i in range(2)]
    o_stg = o_stg2[0]
    o_stgb = o_stgb2[0]
    S0_END = m0.cur
    mt.cur = max(S2_END, S3_END, S0_END)
    o_stg3 = [mt.alloc(f"stg3_{i}", 512, F32) for i in range(2)]
    o_stgb3 = [mt.alloc(f"stgb3_{i}", 512, BF16) for i in range(2)]
    assert mt.cur <= TOT, (S2_END, S3_END, S0_END, mt.cur)

    sched = []

    with (
        nc.sbuf_tensor("big", [128, TOT], U8) as big,
        nc.psum_tensor("ps_a", [128, 512], F32) as ps_a,
        nc.psum_tensor("ps_b", [128, 512], F32) as ps_b,
        nc.psum_tensor("ps_c", [128, 512], F32) as ps_c,
        nc.psum_tensor("ps_d", [128, 512], F32) as ps_d,
        nc.psum_tensor("ps_e", [128, 512], F32) as ps_e,
        nc.psum_tensor("ps_f", [128, 512], F32) as ps_f,
        nc.psum_tensor("ps_g", [128, 512], F32) as ps_g,
        nc.psum_tensor("ps_h", [128, 512], F32) as ps_h,
        nc.semaphore("sd") as sd,
        nc.semaphore("sg") as sg,
        nc.semaphore("spe") as spe,
        nc.semaphore("sa") as sa,
        nc.semaphore("sv") as sv,
        nc.semaphore("sp") as sp,
        nc.semaphore("sc") as sc,
        nc.Block() as blk,
    ):
        SEM = {"sd": sd, "sg": sg, "spe": spe, "sa": sa, "sv": sv, "sp": sp,
               "sc": sc}
        CNT = {k: 0 for k in SEM}
        STEP = {"sd": 16, "sg": 16, "spe": 1, "sa": 1, "sv": 1, "sp": 1,
                "sc": 16}
        ENG_SEM = {"sync": "sd", "gpsimd_dma": "sg", "tensor": "spe",
                   "scalar": "sa", "vector": "sv", "gpsimd": "sp",
                   "scalar_dma": "sc"}

        def V(off, cols, dt_, p0=0, p1=128):
            return big[p0:p1, off:off + cols * ESZ[dt_]].bitcast(dt_)

        def R1(off, p0=0):
            return V(off, NJ, F32, p0, p0 + 1)

        res = {}

        def _wait_list(reads, writes):
            need = {}
            for nm in reads:
                st = res.get(nm)
                if st and st["w"]:
                    s, v = st["w"]
                    need[s] = max(need.get(s, 0), v)
            for nm in writes:
                st = res.get(nm)
                if st:
                    if st["w"]:
                        s, v = st["w"]
                        need[s] = max(need.get(s, 0), v)
                    for s, v in st["r"]:
                        need[s] = max(need.get(s, 0), v)
            return sorted(need.items())

        def emit(eng, body, reads=(), writes=(), extra=(), last=True):
            sem_name = ENG_SEM[eng]
            waits = _wait_list(reads, writes)
            extra = [(s, v) for s, v in extra if v > 0]
            if EMIT_LOG is not None:
                EMIT_LOG.append((eng, ENG_SEM[eng], CNT[ENG_SEM[eng]] + STEP[ENG_SEM[eng]],
                                 tuple(waits), tuple(reads), tuple(writes)))
            CNT[sem_name] += STEP[sem_name]
            val = CNT[sem_name]
            eng_stream = {"gpsimd_dma": "gpsimd",
                          "scalar_dma": "scalar"}.get(eng, eng)

            def fn(e, body=body, waits=tuple(waits), extra=tuple(extra),
                   sem_name=sem_name):
                for s, v in waits:
                    e.wait_ge(SEM[s], v)
                for s, v in extra:
                    e.wait_ge(SEM[s], v)
                r = body(e)
                inc = STEP[sem_name]
                r.then_inc(SEM[sem_name], inc)
            sched.append((eng_stream, fn))
            ev = (sem_name, val)
            for nm in writes:
                res[nm] = {"w": ev, "r": []}
            for nm in reads:
                st = res.setdefault(nm, {"w": None, "r": []})
                st["r"].append(ev)
            return ev

        def raw_wait(eng, ev):
            def fn(e, ev=ev):
                e.wait_ge(SEM[ev[0]], ev[1])
            sched.append((eng, fn))

        tap_ap = {}

        def tap(name, view, reads):
            if name not in TAPS:
                return
            if name not in tap_ap:
                tap_ap[name] = nc.dram_tensor(
                    "tap_" + name, list(view.shape), view.dtype,
                    kind="ExternalOutput").ap()
            emit("sync", lambda e, view=view, name=name: e.dma_start(
                out=tap_ap[name], in_=view), reads=list(reads))

        OVL = (["stg", "stgb", "stgx0", "stgx1"]
               + [f"dt{i}_{j}" for i in range(2 * GRP) for j in range(2)]
               + [f"u{i}_{j}" for i in range(2 * GRP) for j in range(2)]
               + [f"sz{i}_{j}" for i in range(2 * GRP) for j in range(2)]
               + [f"dtu{i}" for i in range(2 * GRP)]
               + [f"accT{i}" for i in range(GRP)]
               + [f"ymE{i}" for i in range(GRP)] + ["ymT"]
               + [f"da{i}" for i in range(4)] + ["bm0"]
               + [f"hh{i}" for i in range(2)]
               + [f"brep{i}" for i in range(2)] + [f"crep{i}" for i in range(2)]
               + [f"xsD{b}{j}" for b in range(2) for j in range(2)]
               + [f"xsP{b}" for b in range(2)]
               + [f"wzxA{b}" for b in range(2)] + [f"wzxB{b}" for b in range(2)]
               + [f"dg{b}" for b in range(2)]
               + ["sp1", "sp20", "sp21", "sp30", "sp31", "yfin", "ptmp"]
               + [f"sa{o}" for o in range(KC)] + ["x1b"]
               + ["vbf", "sqbf", "kr", "zt", "x1e", "x1e0"]
               + [f"rbf{o}" for o in range(KC)] + [f"k2{o}" for o in range(KC)]
               + [f"xr{k}" for k in range(KC)] + [f"xk{k}" for k in range(KC)]
               + ["xfull"] + [f"wst{i}" for i in range(2)])

        def overlay_barrier():
            need = {}
            for nm in OVL:
                st = res.get(nm)
                if not st:
                    continue
                evs = ([st["w"]] if st["w"] else []) + list(st["r"])
                for s, v in evs:
                    need[s] = max(need.get(s, 0), v)
            evs = sorted(need.items())
            for nm in OVL:
                res[nm] = {"w": None, "r": list(evs)}

        emit("sync", lambda e: e.dma_start(out=V(o_A, DC * NS, F32), in_=dAlog),
             writes=["A"])
        for dr_, off, cols, nm in [
            (dD, o_D, DC, "D"), (dbdt, o_bdt, DC, "bdt"),
            (dcw, o_cw, DC * 4, "cw"), (dcb, o_cb, DC, "cb"),
            (dlng, o_lng, KC, "lng"), (dlnb, o_lnb, KC, "lnb"),
            (dmur, o_mur, KC, "mur"), (dmuk, o_muk, KC, "muk"),
        ]:
            emit("sync", lambda e, dr_=dr_, off=off, cols=cols: e.dma_start(
                out=V(off, cols, F32), in_=dr_), writes=[nm])
        emit("scalar", lambda e: e.activation(
            out=V(o_A, DC * NS, F32), in_=V(o_A, DC * NS, F32), func=AF.Exp),
            reads=["A"], writes=["A"])
        emit("vector", lambda e: e.tensor_scalar_mul(
            V(o_A, DC * NS, F32), V(o_A, DC * NS, F32), -1.0),
            reads=["A"], writes=["A"])
        emit("vector", lambda e: e.memset(V(o_onesb, 1, BF16), 1.0),
             writes=["onesb"])
        emit("vector", lambda e: e.memset(V(o_ceps, 1, F32), EPS), writes=["ceps"])
        emit("vector", lambda e: e.memset(V(o_cmh, 1, F32), -0.5), writes=["cmh"])
        emit("vector", lambda e: e.memset(V(o_cm1, 1, F32), -1.0), writes=["cm1"])
        emit("vector", lambda e: e.memset(V(o_c1, 1, F32), 1.0), writes=["c1"])
        emit("vector", lambda e: e.memset(V(o_bcl, 128, F32, 0, 1), 1.0),
             writes=["bcl"])
        emit("vector", lambda e: e.memset(V(o_x1c, KC, F32), 0.0),
             writes=["x1c"])
        emit("vector", lambda e: e.memset(V(o_xsc, DC * 4, F16), 0.0),
             writes=[f"xsc{dc_}" for dc_ in range(DC)])
        emit("vector", lambda e: e.tensor_copy(
            out=V(o_cwb, DC * 4, BF16), in_=V(o_cw, DC * 4, F32)),
            reads=["cw"], writes=["cwb"])

        emit("sync", lambda e: e.dma_start(
            out=V(o_stg2[0], KC * 96, F32)
            .rearrange("p (k c) -> p k c", c=96)[:, :, 0:64],
            in_=dWin[0:DM, 4128:4192].rearrange("(k p) c -> p k c", p=128)),
            writes=["stg0"])
        emit("sync", lambda e: e.dma_start(
            out=V(o_stg2[0], KC * 96, F32)
            .rearrange("p (k c) -> p k c", c=96)[:, :, 64:96],
            in_=dWin[0:DM, 4096:4128].rearrange("(k p) c -> p k c", p=128)),
            writes=["stg0"])
        emit("scalar", lambda e: e.activation(
            out=V(o_wbcd, KC * 96, BF16), in_=V(o_stg2[0], KC * 96, F32),
            func=AF.Copy),
            reads=["stg0"], writes=["wbcd"])
        emit("sync", lambda e: e.dma_start(
            out=V(o_stg2[1], DI, F32, 0, 64), in_=dWdt),
            writes=["stg1"])
        emit("vector", lambda e: e.tensor_copy(
            out=V(o_wdt, DI, BF16, 0, 64), in_=V(o_stg2[1], DI, F32, 0, 64)),
            reads=["stg1"], writes=["wdt"])
        for r in range(DM // 128):
            p = r % 2
            emit("sync", lambda e, r=r, p=p: e.dma_start(
                out=V(o_stg2[p], TH, F32),
                in_=dx[r * 128:(r + 1) * 128, 0:TH]),
                writes=[f"stg{p}"])
            emit("vector", lambda e, r=r, p=p: e.tensor_copy(
                out=V(o_xbh + r * TH * 2, TH, BF16),
                in_=V(o_stg2[p], TH, F32)),
                reads=[f"stg{p}"], writes=["x_bh"])
        for ci, (c0_, tags_) in enumerate([
                (0, ["sWz0", "sWz1", "sWz2", "sWz3"]),
                (2048, ["sWx0", "sWx1", "sWx2", "sWx3"])]):
            p = ci % 2
            emit("scalar_dma", lambda e, c0=c0_, p=p: e.dma_start(
                out=V(o_stg2[p], KC * 512, F32)
                .rearrange("p (k c) -> p k c", c=512),
                in_=dWin[0:DM, c0:c0 + 512]
                .rearrange("(k p) c -> p k c", p=128)),
                writes=[f"stg{p}"])
            emit("vector", lambda e, p=p: e.tensor_copy(
                out=V(o_stgb2[p], KC * 512, BF16),
                in_=V(o_stg2[p], KC * 512, F32)),
                reads=[f"stg{p}"], writes=[f"stgb{p}"])
            emit("gpsimd_dma", lambda e, c0=c0_, p=p: e.dma_start(
                out=sWin[0:DM, c0:c0 + 512]
                .rearrange("(k p) c -> p k c", p=128),
                in_=V(o_stgb2[p], KC * 512, BF16)
                .rearrange("p (k c) -> p k c", c=512)),
                reads=[f"stgb{p}"], writes=tags_)

        late_jobs = []
        lc_ctr = [0]

        def late_stg3(load_fn, store_fn, tags):
            def job(load_fn=load_fn, store_fn=store_fn, tags=tags):
                p = lc_ctr[0] % 2
                lc_ctr[0] += 1
                emit("scalar_dma", lambda e, p=p: load_fn(e, p),
                     writes=[f"stg3_{p}"])
                emit("scalar", lambda e, p=p: e.activation(
                    out=V(o_stgb3[p], 512, BF16),
                    in_=V(o_stg3[p], 512, F32), func=AF.Copy),
                    reads=[f"stg3_{p}"], writes=[f"stgb3_{p}"])
                emit("gpsimd_dma", lambda e, p=p: store_fn(e, p),
                     reads=[f"stgb3_{p}"], writes=tags)
            late_jobs.append(job)

        for dc_ in range(GRP, DC):
            for base, tname in ((0, "sWz"), (2048, "sWx")):
                for cb in range(2):
                    c0_ = base + dc_ * 128 + cb * 64
                    late_stg3(
                        lambda e, p, c0=c0_: e.dma_start(
                            out=V(o_stg3[p], 512, F32)
                            .rearrange("p (k c) -> p k c", c=64),
                            in_=dWin[0:DM, c0:c0 + 64]
                            .rearrange("(k p) c -> p k c", p=128)),
                        lambda e, p, c0=c0_: e.dma_start(
                            out=sWin[0:DM, c0:c0 + 64]
                            .rearrange("(k p) c -> p k c", p=128),
                            in_=V(o_stgb3[p], 512, BF16)
                            .rearrange("p (k c) -> p k c", c=64)),
                        [f"{tname}{dc_}"] if cb == 1 else [])
        for r in range(DM // 128):
            for cb in range(2):
                c0_ = TH + cb * 512
                late_stg3(
                    lambda e, p, r=r, c0=c0_: e.dma_start(
                        out=V(o_stg3[p], 512, F32),
                        in_=dx[r * 128:(r + 1) * 128, c0:c0 + 512]),
                    lambda e, p, r=r, c0=c0_: e.dma_start(
                        out=sXbf[r * 128:(r + 1) * 128, c0:c0 + 512],
                        in_=V(o_stgb3[p], 512, BF16)),
                    ["sXbf1"] if (r == DM // 128 - 1 and cb == 1) else [])

        def late_cast(dsrc, ddst, rows, tag):
            nchunks = rows // 128
            for r in range(nchunks):
                for cb in range(2):
                    late_stg3(
                        lambda e, p, r=r, cb=cb, dsrc=dsrc: e.dma_start(
                            out=V(o_stg3[p], 512, F32),
                            in_=dsrc[r * 128:(r + 1) * 128,
                                     cb * 512:(cb + 1) * 512]),
                        lambda e, p, r=r, cb=cb, ddst=ddst: e.dma_start(
                            out=ddst[r * 128:(r + 1) * 128,
                                     cb * 512:(cb + 1) * 512],
                            in_=V(o_stgb3[p], 512, BF16)),
                        [tag] if (r == nchunks - 1 and cb == 1) else [])

        late_cast(dWout, sWout, DI, "sWout")
        late_cast(dWr, sWr, DM, "sWr")
        late_cast(dWk, sWk, DM, "sWk")
        late_cast(dWv, sWv, DM, "sWv")

        emit("vector", lambda e: e.memset(V(o_stgb2[0], 4096, BF16), 0.0),
             writes=["stgb0"])
        for i in range(2):
            emit("sync", lambda e, i=i: e.dma_start(
                out=sDiag[i * 4096:(i + 1) * 4096, :]
                .rearrange("(p a) c -> p (a c)", p=128),
                in_=V(o_stgb2[0], 4096, BF16)),
                reads=["stgb0"], writes=[f"diagz{i}"])
        sDiag_f = sDiag.rearrange("a b -> (a b)")
        for idx in range(DC * 4):
            dst = sDiag_f[idx * 16384:(idx + 1) * 16384][0:16384:129]
            emit("sync", lambda e, dst=dst, idx=idx: e.dma_start(
                out=dst, in_=V(o_cwb, DC * 4, BF16)[:, idx:idx + 1],
                allow_slow_non_contiguous=True),
                reads=["cwb", "diagz0", "diagz1"],
                writes=["diag"] if idx == DC * 4 - 1 else [])

        PS = {"a": ps_a, "b": ps_b, "c": ps_c, "d": ps_d, "e": ps_e,
              "f": ps_f, "g": ps_g, "h": ps_h}

        def mm(e, psm, lhsT, rhs, start, stop):
            return e.matmul(psm, lhsT, rhs, start=start, stop=stop)

        for h in range(2):
            hb = h * TH
            overlay_barrier()

            def emit_s1(h_, hb_):
                if h_ == 1:
                    emit("sync", lambda e, hb=hb_: e.dma_start(
                        out=V(o_xbh, KC * TH, BF16)
                        .rearrange("p (k t) -> p k t", t=TH),
                        in_=sXbf[0:DM, hb:hb + TH]
                        .rearrange("(k p) t -> p k t", p=128)),
                        reads=["sXbf1"], writes=["x_bh"])
                for j in range(2):
                    pnm = "ps_" + ("a" if j == 0 else "b")
                    psm = PS[pnm[3]]

                    def mm_bcd(e, j=j, psm=psm):
                        last = None
                        for kc in range(KC):
                            last = mm(e, psm[0:96, :],
                                      V(o_wbcd + kc * 96 * 2, 96, BF16),
                                      V(o_xbh + kc * TH * 2 + (j * NJ) * 2,
                                        NJ, BF16),
                                      kc == 0, kc == KC - 1)
                        return last
                    emit("tensor", mm_bcd,
                         reads=["wbcd", "x_bh"],
                         writes=[pnm])
                    emit("scalar", lambda e, j=j, psm=psm: e.activation(
                        out=V(o_dtin + j * NJ * 2, NJ, BF16, 0, 64),
                        in_=psm[0:64, :], func=AF.Copy),
                        reads=[pnm], writes=[f"dtin{j}"])
                    emit("scalar", lambda e, j=j, psm=psm: e.activation(
                        out=V(o_bcsb + j * NJ * 2, NJ, F16, 64, 96),
                        in_=psm[64:96, :], func=AF.Copy),
                        reads=[pnm], writes=[f"bcsb{j}"])
                emit("sync", lambda e: e.dma_start(
                    out=sBC, in_=V(o_bcsb, TH, F16, 64, 96)),
                    reads=["bcsb0", "bcsb1"], writes=["sBC"])
            if h == 0:
                emit_s1(0, 0)

            unit = [0]
            ev_scan, ev_ym = {}, {}
            NGRP = DC // GRP
            usched = [((g_ % 2) * GRP + i0_, g_ * GRP + i0_, n_)
                      for g_ in range(NGRP) for n_ in range(NS)
                      for i0_ in range(GRP)]
            nxe = [0]

            def drain_exps(upto):
                while nxe[0] <= upto and nxe[0] < len(usched):
                    ue = nxe[0]
                    nxe[0] += 1
                    ie, dce, ne = usched[ue]
                    emit("scalar", lambda e, i=ie, ub=ue % 6, dc=dce,
                         n=ne: e.activation(
                             out=V(o_da[ub], TH, F16),
                             in_=V(o_dt[i], TH, F16), func=AF.Exp,
                             scale=V(o_A, DC * NS, F32)
                             [:, dc * NS + n:dc * NS + n + 1]),
                         reads=[f"dt{i}_0", f"dt{i}_1", "A"],
                         writes=[f"da{ue % 6}"])

            def emit_prod(g, only_i0=None, part=None, h=h, hb=hb):
                dcs = [g * GRP + i for i in range(GRP)]
                gp = (g % 2) * GRP
                for i0, dc in enumerate(dcs):
                    if part == 2 or (only_i0 is not None and i0 != only_i0):
                        continue
                    i = gp + i0
                    bf = i0 % 2
                    emit("sync", lambda e, dc=dc, bf=bf: e.dma_start(
                        out=V(o_wzx[bf], KC * 128, BF16)
                        .rearrange("p (k c) -> p k c", c=128),
                        in_=sWin[0:DM, dc * 128:(dc + 1) * 128]
                        .rearrange("(k p) c -> p k c", p=128)),
                        reads=[f"sWz{dc}"], writes=[f"wzxA{bf}"])
                    emit("sync", lambda e, dc=dc, bf=bf: e.dma_start(
                        out=V(o_wzx[bf] + KC * 128 * 2, KC * 128, BF16)
                        .rearrange("p (k c) -> p k c", c=128),
                        in_=sWin[0:DM, 2048 + dc * 128:2048 + (dc + 1) * 128]
                        .rearrange("(k p) c -> p k c", p=128)),
                        reads=[f"sWx{dc}"], writes=[f"wzxB{bf}"])
                    emit("sync", lambda e, dc=dc, bf=bf: e.dma_start(
                        out=V(o_dg[bf], 4 * 128, BF16)
                        .rearrange("p (f c) -> p f c", c=128),
                        in_=sDiag[dc * 512:(dc + 1) * 512, :]
                        .rearrange("(f p) c -> p f c", p=128)),
                        reads=["diag"], writes=[f"dg{bf}"])
                    for j in range(2):
                        pnm = "ps_" + ("a" if j == 0 else "b")
                        psm = PS[pnm[3]]

                        def mm_xs(e, bf=bf, j=j, psm=psm):
                            last = None
                            for kc in range(KC):
                                last = mm(e, psm[:],
                                          V(o_wzx[bf] + kc * 128 * 2, 128, BF16),
                                          V(o_xbh + kc * TH * 2 + (j * NJ) * 2,
                                            NJ, BF16),
                                          kc == 0, kc == KC - 1)
                            return last
                        emit("tensor", mm_xs, reads=[f"wzxA{bf}", "x_bh"],
                             writes=[pnm])
                        emit("scalar", lambda e, bf=bf, j=j, psm=psm: e.activation(
                            out=V(o_xs[bf] + (3 + j * NJ) * 2, NJ, BF16),
                            in_=psm[:], func=AF.Copy),
                            reads=[pnm], writes=[f"xsD{bf}{j}"])
                    emit("scalar", lambda e, bf=bf, dc=dc: e.activation(
                        out=V(o_xs[bf], 3, BF16),
                        in_=V(o_xsc + dc * 4 * 2, 3, BF16), func=AF.Copy),
                        reads=[f"xsc{dc}"], writes=[f"xsP{bf}"])
                    if h == 0:
                        emit("scalar", lambda e, bf=bf, dc=dc: e.activation(
                            out=V(o_xsc + dc * 4 * 2, 3, BF16),
                            in_=V(o_xs[bf] + (3 + TH - 3) * 2, 3, BF16),
                            func=AF.Copy),
                            reads=[f"xsD{bf}1"], writes=[f"xsc{dc}"])
                    for j in range(2):
                        pnm = "ps_" + ("c" if j == 0 else "d")
                        psm = PS[pnm[3]]

                        def mm_cv(e, bf=bf, j=j, psm=psm):
                            last = None
                            for tp in range(4):
                                last = mm(e, psm[:],
                                          V(o_dg[bf] + tp * 128 * 2, 128, BF16),
                                          V(o_xs[bf] + (j * NJ + tp) * 2, NJ, BF16),
                                          tp == 0, tp == 3)
                            return last
                        emit("tensor", mm_cv,
                             reads=[f"dg{bf}", f"xsD{bf}0", f"xsD{bf}1",
                                    f"xsP{bf}"],
                             writes=[pnm])
                        emit("scalar", lambda e, i=i, j=j, psm=psm, dc=dc:
                             e.activation(
                                 out=V(o_u[i] + j * NJ * 2, NJ, F16),
                                 in_=psm[:], func=AF.Silu,
                                 bias=V(o_cb, DC, F32)[:, dc:dc + 1]),
                             reads=[pnm, "cb"], writes=[f"u{i}_{j}"])
                    for j in range(2):
                        pnm = "ps_" + ("e" if j == 0 else "f")
                        psm = PS[pnm[3]]

                        def mm_z(e, bf=bf, j=j, psm=psm):
                            last = None
                            for kc in range(KC):
                                last = mm(e, psm[:],
                                          V(o_wzx[bf] + (KC + kc) * 128 * 2, 128,
                                            BF16),
                                          V(o_xbh + kc * TH * 2 + (j * NJ) * 2,
                                            NJ, BF16),
                                          kc == 0, kc == KC - 1)
                            return last
                        emit("tensor", mm_z, reads=[f"wzxB{bf}", "x_bh"],
                             writes=[pnm])
                        emit("scalar", lambda e, i=i, j=j, psm=psm: e.activation(
                            out=V(o_sz[i] + j * NJ * 2, NJ, F16),
                            in_=psm[:], func=AF.Silu),
                            reads=[pnm], writes=[f"sz{i}_{j}"])
                for i0, dc in enumerate(dcs):
                    if part == 1:
                        continue
                    i = gp + i0
                    for j in range(2):
                        pnm = "ps_" + ("g" if (i0 % 2) == 0 else "h")
                        psm = PS[pnm[3]]
                        sj = i0 % 2
                        emit("tensor", lambda e, dc=dc, j=j, psm=psm: mm(
                            e, psm[0:128, j * 0:512] if False else psm[:],
                            V(o_wdt + dc * 128 * 2, 128, BF16, 0, 64),
                            V(o_dtin + j * NJ * 2, NJ, BF16, 0, 64),
                            True, True),
                            reads=["wdt", f"dtin{j}"], writes=[pnm])
                        emit("scalar", lambda e, psm=psm, dc=dc: e.activation(
                            out=V(o_sp1, NJ, F32), in_=psm[:], func=AF.Abs,
                            bias=V(o_bdt, DC, F32)[:, dc:dc + 1]),
                            reads=[pnm, "bdt"], writes=["sp1"])
                        emit("scalar", lambda e, sj=sj: e.activation(
                            out=V(o_sp2[sj], NJ, F16), in_=V(o_sp1, NJ, F32),
                            func=AF.Exp, scale=V(o_cm1, 1, F32)),
                            reads=["sp1", "cm1"], writes=[f"sp2{sj}"])
                        emit("scalar", lambda e, sj=sj: e.activation(
                            out=V(o_sp3[sj], NJ, F16), in_=V(o_sp2[sj], NJ, F16),
                            func=AF.Ln, bias=V(o_c1, 1, F32)),
                            reads=[f"sp2{sj}", "c1"], writes=[f"sp3{sj}"])
                        emit("scalar", lambda e, psm=psm, dc=dc, sj=sj: e.activation(
                            out=V(o_sp2[sj], NJ, F16), in_=psm[:], func=AF.Relu,
                            bias=V(o_bdt, DC, F32)[:, dc:dc + 1]),
                            reads=[pnm, "bdt"], writes=[f"sp2{sj}"])
                        emit("vector", lambda e, i=i, j=j, sj=sj: e.tensor_tensor(
                            V(o_dt[i] + j * NJ * 2, NJ, F16),
                            V(o_sp3[sj], NJ, F16), V(o_sp2[sj], NJ, F16), ALU.add),
                            reads=[f"sp2{sj}", f"sp3{sj}"], writes=[f"dt{i}_{j}"])
                    emit("vector", lambda e, i=i: e.tensor_tensor(
                        V(o_dtu[i], TH, F16), V(o_dt[i], TH, F16),
                        V(o_u[i], TH, F16), ALU.mult),
                        reads=[f"dt{i}_0", f"dt{i}_1", f"u{i}_0", f"u{i}_1"],
                        writes=[f"dtu{i}"])
                    tap(f"xs_h{h}_dc{dc}", V(o_xs[i0 % 2], TH + 4, BF16),
                        [f"xsD{i0 % 2}0", f"xsD{i0 % 2}1", f"xsP{i0 % 2}"])
                    tap(f"u_h{h}_dc{dc}", V(o_u[i], TH, F16),
                        [f"u{i}_0", f"u{i}_1"])
                    tap(f"sz_h{h}_dc{dc}", V(o_sz[i], TH, F16),
                        [f"sz{i}_0", f"sz{i}_1"])
                    tap(f"dt_h{h}_dc{dc}", V(o_dt[i], TH, F16),
                        [f"dt{i}_0", f"dt{i}_1"])
                    tap(f"dtu_h{h}_dc{dc}", V(o_dtu[i], TH, F16), [f"dtu{i}"])


            emit_prod(0)
            for g in range(NGRP):
                dcs = [g * GRP + i for i in range(GRP)]
                gp = (g % 2) * GRP
                for n in range(NS):
                    pb = n % 2
                    emit("sync", lambda e, pb=pb, n=n: e.dma_start(
                        out=V(o_brep[pb], TH, F16),
                        in_=sBC[n:n + 1, :].broadcast_to([128, TH])),
                        reads=["sBC"], writes=[f"brep{pb}"])
                    emit("sync", lambda e, pb=pb, n=n: e.dma_start(
                        out=V(o_crep[pb], TH, F16),
                        in_=sBC[16 + n:17 + n, :].broadcast_to([128, TH])),
                        reads=["sBC"], writes=[f"crep{pb}"])
                    if h == 0:
                        for _ in range(3):
                            if late_jobs:
                                late_jobs.pop(0)()
                    if n == 10 and g + 1 < NGRP:
                        emit_prod(g + 1, part=1)
                    if n == 13 and g + 1 < NGRP:
                        emit_prod(g + 1, part=2)
                    for i0, dc in enumerate(dcs):
                        i = gp + i0
                        u_ = unit[0]
                        unit[0] += 1
                        drain_exps(u_ + 5)
                        ub = u_ % 6
                        emit("vector", lambda e, i=i, pb=pb:
                             e.tensor_tensor(
                                 V(o_bm[0], TH, F16), V(o_dtu[i], TH, F16),
                                 V(o_brep[pb], TH, F16), ALU.mult),
                             reads=[f"dtu{i}", f"brep{pb}"], writes=["bm0"])

                        hb_ = u_ % 4

                        def f_scan(e, ub=ub, hb_=hb_, dc=dc, n=n, h=h):
                            init = (0.0 if h == 0 else
                                    V(o_carry, DC * NS, F32)
                                    [:, dc * NS + n:dc * NS + n + 1])
                            return e.tensor_tensor_scan(
                                V(o_hh[hb_], TH, F16), V(o_da[ub], TH, F16),
                                V(o_bm[0], TH, F16), init, ALU.mult, ALU.add)
                        ev_scan[u_] = emit(
                            "vector", f_scan,
                            reads=[f"da{ub}", "bm0"] +
                                  (["carry"] if h == 1 else []),
                            writes=[f"hh{hb_}"])
                        tap(f"h_h{h}_dc{dc}_n{n}", V(o_hh[hb_], TH, F16),
                            [f"hh{hb_}"])
                        tap(f"da_h{h}_dc{dc}_n{n}", V(o_da[ub], TH, F16),
                            [f"da{ub}"])
                        tap(f"bm_h{h}_dc{dc}_n{n}", V(o_bm[0], TH, F16),
                            ["bm0"])
                        if h == 0:
                            emit("scalar", lambda e, hb_=hb_, dc=dc, n=n:
                                 e.activation(
                                     out=V(o_carry, DC * NS, F32)
                                     [:, dc * NS + n:dc * NS + n + 1],
                                     in_=V(o_hh[hb_] + (TH - 1) * 2, 1, F16),
                                     func=AF.Copy),
                                 reads=[f"hh{hb_}"], writes=["carry_el"])
                        if n % 2 == 0:
                            emit("vector", lambda e, hb_=hb_, pb=pb, i0=i0:
                                 e.tensor_tensor(
                                     V(o_ymE[i0], TH, F16), V(o_hh[hb_], TH, F16),
                                     V(o_crep[pb], TH, F16), ALU.mult),
                                 reads=[f"hh{hb_}", f"crep{pb}"],
                                 writes=[f"ymE{i0}"])
                        else:
                            emit("vector", lambda e, hb_=hb_, pb=pb:
                                 e.tensor_tensor(
                                     V(o_ymT, TH, F16), V(o_hh[hb_], TH, F16),
                                     V(o_crep[pb], TH, F16), ALU.mult),
                                 reads=[f"hh{hb_}", f"crep{pb}"], writes=["ymT"])
                            emit("vector", lambda e, i0=i0: e.tensor_tensor(
                                V(o_ptmp, TH, F16), V(o_ymE[i0], TH, F16),
                                V(o_ymT, TH, F16), ALU.add),
                                reads=[f"ymE{i0}", "ymT"], writes=["ptmp", "bm0"])
                            if n == 1:
                                emit("vector", lambda e, i0=i0: e.tensor_copy(
                                    out=V(o_acc[i0], TH, F16),
                                    in_=V(o_ptmp, TH, F16)),
                                    reads=["ptmp", "bm0"], writes=[f"accT{i0}"])
                            else:
                                emit("vector", lambda e, i0=i0: e.tensor_tensor(
                                    V(o_acc[i0], TH, F16), V(o_acc[i0], TH, F16),
                                    V(o_ptmp, TH, F16), ALU.add),
                                    reads=["ptmp", "bm0", f"accT{i0}"],
                                    writes=[f"accT{i0}"])
                for i0, dc in enumerate(dcs):
                    i = gp + i0
                    emit("vector", lambda e, i=i, i0=i0, dc=dc:
                         e.scalar_tensor_tensor(
                             V(o_yfin, TH, F16), V(o_u[i], TH, F16),
                             V(o_D, DC, F32)[:, dc:dc + 1], V(o_acc[i0], TH, F16),
                             ALU.mult, ALU.add),
                         reads=[f"accT{i0}", f"u{i}_0", f"u{i}_1", "D"],
                         writes=["yfin", "bm0"])
                    emit("vector", lambda e, i=i, dc=dc: e.tensor_tensor(
                        V(o_yg + dc * TH * 2, TH, BF16), V(o_yfin, TH, F16),
                        V(o_sz[i], TH, F16), ALU.mult),
                        reads=["yfin", "bm0", f"sz{i}_0", f"sz{i}_1"],
                        writes=[f"yg{dc}"])
                    tap(f"yg_h{h}_dc{dc}", V(o_yg + dc * TH * 2, TH, BF16),
                        [f"yg{dc}"])

            while late_jobs:
                late_jobs.pop(0)()
            if h == 0:
                emit_s1(1, TH)
            overlay_barrier()
            for j in range(2):
                jb = j * NJ
                emit("sync", lambda e, hb=hb, jb=jb: e.dma_start(
                    out=V(o_xfull, KC * NJ, F32)
                    .rearrange("p (o t) -> p o t", t=NJ),
                    in_=dx[0:DM, hb + jb:hb + jb + NJ]
                    .rearrange("(o p) t -> p o t", p=128)),
                    writes=["xfull"])
                ev_attn = {}
                for oc in range(KC):
                    wb = oc % 2
                    emit("sync", lambda e, oc=oc, wb=wb: e.dma_start(
                        out=V(o_wst[wb], 16 * 128, BF16)
                        .rearrange("p (k c) -> p k c", c=128),
                        in_=sWout[0:DI, oc * 128:(oc + 1) * 128]
                        .rearrange("(k p) c -> p k c", p=128)),
                        reads=["sWout"], writes=[f"wst{wb}"])
                    pnm = "ps_" + ("a" if oc % 2 == 0 else "b")
                    psm = PS[pnm[3]]

                    def mm_wo(e, wb=wb, psm=psm, jb=jb):
                        last = None
                        for kc in range(DC):
                            last = mm(e, psm[:],
                                      V(o_wst[wb] + kc * 128 * 2, 128, BF16),
                                      V(o_yg + kc * TH * 2 + jb * 2, NJ, BF16),
                                      kc == 0, kc == DC - 1)
                        return last
                    emit("tensor", mm_wo,
                         reads=[f"wst{wb}"] + [f"yg{d}" for d in range(DC)],
                         writes=[pnm])
                    emit("vector", lambda e, oc=oc, psm=psm:
                         e.scalar_tensor_tensor(
                             V(o_sa_ + oc * NJ * 4, NJ, F32), psm[:], 1.0,
                             V(o_xfull + oc * NJ * 4, NJ, F32),
                             ALU.mult, ALU.add),
                         reads=[pnm, "xfull"], writes=[f"sa{oc}"])

                SA = [f"sa{o}" for o in range(KC)]

                def bview(psm):
                    return psm[:].unsqueeze(1).broadcast_to([128, KC, NJ])

                def batched(off, dt_):
                    return V(off, KC * NJ, dt_).rearrange(
                        "p (o t) -> p o t", t=NJ)

                def ln_stats(tag):
                    emit("vector", lambda e: e.tensor_copy(
                        out=V(o_vbf, KC * NJ, BF16),
                        in_=V(o_sa_, KC * NJ, F32)),
                        reads=SA, writes=["vbf"])
                    emit("scalar", lambda e: e.activation(
                        out=V(o_sqbf, KC * NJ, BF16),
                        in_=V(o_sa_, KC * NJ, F32), func=AF.Square),
                        reads=SA, writes=["sqbf"])

                    def mm_sums(e):
                        last = None
                        for kc in range(KC):
                            mm(e, ps_c[0:1, :], V(o_onesb, 1, BF16),
                               V(o_vbf + kc * NJ * 2, NJ, BF16),
                               kc == 0, kc == KC - 1)
                        for kc in range(KC):
                            last = mm(e, ps_c[32:33, :], V(o_onesb, 1, BF16),
                                      V(o_sqbf + kc * NJ * 2, NJ, BF16),
                                      kc == 0, kc == KC - 1)
                        return last
                    emit("tensor", mm_sums, reads=["vbf", "sqbf", "onesb"],
                         writes=["ps_c"])
                    emit("vector", lambda e: e.tensor_scalar_mul(
                        R1(o_row1), ps_c[0:1, :], 1.0 / DM),
                        reads=["ps_c"], writes=["rowa"])
                    emit("vector", lambda e: e.tensor_tensor(
                        R1(o_row1, 32), R1(o_row1), R1(o_row1), ALU.mult),
                        reads=["rowa"], writes=["rowd"])
                    emit("vector", lambda e: e.scalar_tensor_tensor(
                        R1(o_row1, 64), ps_c[32:33, :], 1.0 / DM,
                        R1(o_row1, 32), ALU.mult, ALU.subtract),
                        reads=["ps_c", "rowd"], writes=["rowe"])
                    emit("scalar", lambda e: e.activation(
                        out=R1(o_row1, 32), in_=R1(o_row1, 64), func=AF.Ln,
                        bias=V(o_ceps, 1, F32, 0, 1)),
                        reads=["rowe", "ceps"], writes=["rowd"])
                    emit("scalar", lambda e: e.activation(
                        out=R1(o_row2), in_=R1(o_row1, 32), func=AF.Exp,
                        scale=V(o_cmh, 1, F32, 0, 1)),
                        reads=["rowd", "cmh"], writes=["rowb"])
                    emit("vector", lambda e: e.tensor_tensor(
                        R1(o_row3), R1(o_row1), R1(o_row2), ALU.mult),
                        reads=["rowa", "rowb"], writes=["rowc"])

                    def mm_bc(e):
                        mm(e, ps_d[:], V(o_bcl, 128, F32, 0, 1), R1(o_row2),
                           True, True)
                        return mm(e, ps_e[:], V(o_bcl, 128, F32, 0, 1),
                                  R1(o_row3), True, True)
                    emit("tensor", mm_bc, reads=["rowb", "rowc", "bcl"],
                         writes=["ps_d", "ps_e"])

                ln_stats("ln0")
                emit("vector", lambda e: e.tensor_tensor(
                    batched(o_x1, F32), batched(o_sa_, F32), bview(ps_d),
                    ALU.mult),
                    reads=SA + ["ps_d"], writes=["x1b"])
                emit("vector", lambda e: e.tensor_tensor(
                    batched(o_x1, F32), batched(o_x1, F32), bview(ps_e),
                    ALU.subtract),
                    reads=["x1b", "ps_e"], writes=["x1b"])
                for oc in range(KC):
                    emit("vector", lambda e, oc=oc: e.scalar_tensor_tensor(
                        V(o_sa_ + oc * NJ * 4, NJ, F32),
                        V(o_x1 + oc * NJ * 4, NJ, F32),
                        V(o_lng, KC, F32)[:, oc:oc + 1],
                        V(o_xfull + oc * NJ * 4, NJ, F32),
                        ALU.mult, ALU.add),
                        reads=["x1b", "lng", "xfull"], writes=[f"sa{oc}"])
                    emit("vector", lambda e, oc=oc: e.tensor_scalar_add(
                        V(o_sa_ + oc * NJ * 4, NJ, F32),
                        V(o_sa_ + oc * NJ * 4, NJ, F32),
                        V(o_lnb, KC, F32)[:, oc:oc + 1]),
                        reads=["lnb", f"sa{oc}"], writes=[f"sa{oc}"])
                for oc in range(KC):
                    tap(f"attnr1_h{h}_j{j}_oc{oc}",
                        V(o_sa_ + oc * NJ * 4, NJ, F32), [f"sa{oc}"])
                ln_stats("ln1")
                emit("vector", lambda e: e.tensor_tensor(
                    batched(o_x1, F32), batched(o_sa_, F32), bview(ps_d),
                    ALU.mult),
                    reads=SA + ["ps_d"], writes=["x1b"])
                emit("vector", lambda e: e.tensor_tensor(
                    batched(o_x1, F32), batched(o_x1, F32), bview(ps_e),
                    ALU.subtract),
                    reads=["x1b", "ps_e"], writes=["x1b"])
                for oc in range(KC):
                    tap(f"x1_h{h}_j{j}_oc{oc}",
                        V(o_x1 + oc * NJ * 4, NJ, F32), ["x1b"])

                for kc in range(KC):
                    emit("vector", lambda e, kc=kc: e.tensor_copy(
                        out=V(o_x1e + 4, NJ, F32),
                        in_=V(o_x1 + kc * NJ * 4, NJ, F32)),
                        reads=["x1b"], writes=["x1e"])
                    emit("vector", lambda e, kc=kc: e.tensor_copy(
                        out=V(o_x1e, 1, F32), in_=V(o_x1c + kc * 4, 1, F32)),
                        reads=["x1c"], writes=["x1e0"])
                    emit("vector", lambda e, kc=kc: e.tensor_copy(
                        out=V(o_x1c + kc * 4, 1, F32),
                        in_=V(o_x1 + kc * NJ * 4 + (NJ - 1) * 4, 1, F32)),
                        reads=["x1b"], writes=["x1c"])
                    emit("vector", lambda e: e.tensor_tensor(
                        V(o_zt, NJ, F32), V(o_x1e + 4, NJ, F32),
                        V(o_x1e, NJ, F32), ALU.subtract),
                        reads=["x1e", "x1e0"], writes=["zt"])
                    emit("vector", lambda e, kc=kc: e.scalar_tensor_tensor(
                        V(o_xr + kc * NJ * 2, NJ, BF16), V(o_zt, NJ, F32),
                        V(o_mur, KC, F32)[:, kc:kc + 1], V(o_x1e, NJ, F32),
                        ALU.mult, ALU.add),
                        reads=["zt", "mur", "x1e", "x1e0"], writes=[f"xr{kc}"])
                    emit("vector", lambda e, kc=kc: e.scalar_tensor_tensor(
                        V(o_xk + kc * NJ * 2, NJ, BF16), V(o_zt, NJ, F32),
                        V(o_muk, KC, F32)[:, kc:kc + 1], V(o_x1e, NJ, F32),
                        ALU.mult, ALU.add),
                        reads=["zt", "muk", "x1e", "x1e0"], writes=[f"xk{kc}"])

                for oc in range(KC):
                    wb = (oc // 2) % 2
                    if oc % 2 == 0:
                        emit("sync", lambda e, oc=oc, wb=wb: e.dma_start(
                            out=V(o_wst[wb], 16 * 128, BF16)
                            .rearrange("p (k m) -> p k m", m=256),
                            in_=sWr[0:DM, oc * 128:(oc + 2) * 128]
                            .rearrange("(k p) m -> p k m", p=128)),
                            reads=["sWr"], writes=[f"wst{wb}"])

                    pr = [ps_f, ps_d][oc % 2]
                    prn = ["ps_f", "ps_d"][oc % 2]

                    ho = (oc % 2) * 128 * 2

                    def mm_r(e, wb=wb, pr=pr, ho=ho):
                        last = None
                        for kc in range(KC):
                            last = mm(e, pr[:],
                                      V(o_wst[wb] + ho + kc * 256 * 2, 128, BF16),
                                      V(o_xr + kc * NJ * 2, NJ, BF16),
                                      kc == 0, kc == KC - 1)
                        return last
                    emit("tensor", mm_r,
                         reads=[f"wst{wb}"] + [f"xr{k}" for k in range(KC)],
                         writes=[prn])
                    emit("scalar", lambda e, oc=oc, pr=pr: e.activation(
                        out=V(o_rbf + oc * NJ * 2, NJ, BF16), in_=pr[:],
                        func=AF.Sigmoid),
                        reads=[prn], writes=[f"rbf{oc}"])
                for oc in range(KC):
                    wb = (oc // 2) % 2
                    if oc % 2 == 0:
                        emit("sync", lambda e, oc=oc, wb=wb: e.dma_start(
                            out=V(o_wst[wb], 16 * 128, BF16)
                            .rearrange("p (k m) -> p k m", m=256),
                            in_=sWk[0:DM, oc * 128:(oc + 2) * 128]
                            .rearrange("(k p) m -> p k m", p=128)),
                            reads=["sWk"], writes=[f"wst{wb}"])

                    pk = [ps_g, ps_e][oc % 2]
                    pkn = ["ps_g", "ps_e"][oc % 2]

                    ho = (oc % 2) * 128 * 2

                    def mm_k(e, wb=wb, pk=pk, ho=ho):
                        last = None
                        for kc in range(KC):
                            last = mm(e, pk[:],
                                      V(o_wst[wb] + ho + kc * 256 * 2, 128, BF16),
                                      V(o_xk + kc * NJ * 2, NJ, BF16),
                                      kc == 0, kc == KC - 1)
                        return last
                    emit("tensor", mm_k,
                         reads=[f"wst{wb}"] + [f"xk{k}" for k in range(KC)],
                         writes=[pkn])
                    emit("scalar", lambda e, pk=pk: e.activation(
                        out=V(o_kr, NJ, BF16), in_=pk[:], func=AF.Relu),
                        reads=[pkn], writes=["kr"])
                    emit("vector", lambda e, oc=oc: e.tensor_tensor(
                        V(o_k2bf + oc * NJ * 2, NJ, BF16), V(o_kr, NJ, BF16),
                        V(o_kr, NJ, BF16), ALU.mult),
                        reads=["kr"], writes=[f"k2{oc}"])
                for oc in range(KC):
                    wb = (oc // 2) % 2
                    if oc % 2 == 0:
                        emit("sync", lambda e, oc=oc, wb=wb: e.dma_start(
                            out=V(o_wst[wb], 16 * 128, BF16)
                            .rearrange("p (k m) -> p k m", m=256),
                            in_=sWv[0:DM, oc * 128:(oc + 2) * 128]
                            .rearrange("(k p) m -> p k m", p=128)),
                            reads=["sWv"], writes=[f"wst{wb}"])

                    pv = [ps_h, ps_g][oc % 2]
                    pvn = ["ps_h", "ps_g"][oc % 2]

                    ho = (oc % 2) * 128 * 2

                    def mm_v(e, wb=wb, pv=pv, ho=ho):
                        last = None
                        for kc in range(KC):
                            last = mm(e, pv[:],
                                      V(o_wst[wb] + ho + kc * 256 * 2, 128, BF16),
                                      V(o_k2bf + kc * NJ * 2, NJ, BF16),
                                      kc == 0, kc == KC - 1)
                        return last
                    emit("tensor", mm_v,
                         reads=[f"wst{wb}"] + [f"k2{k}" for k in range(KC)],
                         writes=[pvn])
                    emit("vector", lambda e, oc=oc, pv=pv: e.tensor_tensor(
                        V(o_zt, NJ, F32), V(o_rbf + oc * NJ * 2, NJ, BF16),
                        pv[:], ALU.mult),
                        reads=[pvn, f"rbf{oc}"], writes=["zt"])
                    emit("vector", lambda e, oc=oc: e.tensor_tensor(
                        V(o_sa_ + oc * NJ * 4, NJ, F32),
                        V(o_x1 + oc * NJ * 4, NJ, F32), V(o_zt, NJ, F32),
                        ALU.add),
                        reads=["zt", "x1b"], writes=[f"sa{oc}"])
                ln_stats("ln2")
                emit("vector", lambda e: e.tensor_tensor(
                    batched(o_x1, F32), batched(o_sa_, F32), bview(ps_d),
                    ALU.mult),
                    reads=SA + ["ps_d"], writes=["x1b"])
                emit("vector", lambda e: e.tensor_tensor(
                    batched(o_x1, F32), batched(o_x1, F32), bview(ps_e),
                    ALU.subtract),
                    reads=["x1b", "ps_e"], writes=["x1b"])
                emit("gpsimd_dma", lambda e, hb=hb, jb=jb: e.dma_start(
                    out=dy[0:DM, hb + jb:hb + jb + NJ]
                    .rearrange("(o p) t -> p o t", p=128),
                    in_=batched(o_x1, F32)),
                    reads=["x1b"], writes=[])

        def fin(e):
            for s in ("sd", "sg", "spe", "sa", "sv", "sp", "sc"):
                if CNT[s]:
                    e.wait_ge(SEM[s], CNT[s])
        sched.append(("gpsimd", fin))

        @blk.sync
        def _(eng):
            for e, fn in sched:
                if e == "sync":
                    fn(eng)

        @blk.tensor
        def _(eng):
            for e, fn in sched:
                if e == "tensor":
                    fn(eng)

        @blk.scalar
        def _(eng):
            for e, fn in sched:
                if e == "scalar":
                    fn(eng)

        @blk.vector
        def _(eng):
            for e, fn in sched:
                if e == "vector":
                    fn(eng)

        @blk.gpsimd
        def _(eng):
            for e, fn in sched:
                if e == "gpsimd":
                    fn(eng)

    mybir.codegen_inst_isa_subclasses(nc)
    return nc


def _prep_inputs(inputs):
    f = np.ascontiguousarray
    shared = {
        "w_in_t": f(inputs["L0_attn_W_in"].T),
        "w_dt_t": f(inputs["L0_attn_W_dt"].T),
        "w_out_t": f(inputs["L0_attn_W_out"].T),
        "w_r_t": f(inputs["L0_ffn_W_r"].T),
        "w_k_t": f(inputs["L0_ffn_W_k"].T),
        "w_v_t": f(inputs["L0_ffn_W_v"].T),
        "a_log_t": f(inputs["L0_attn_A_log"].reshape(DC, 128, NS)
                     .transpose(1, 0, 2).reshape(128, DC * NS)),
        "dd_t": f(inputs["L0_attn_D"].reshape(DC, 128).T),
        "bdt_t": f(inputs["L0_attn_b_dt"].reshape(DC, 128).T),
        "convw_t": f(inputs["L0_attn_conv_w"].reshape(DC, 128, 4)
                     .transpose(1, 0, 2).reshape(128, DC * 4)),
        "convb_t": f(inputs["L0_attn_conv_b"].reshape(DC, 128).T),
        "lng_t": f(inputs["L0_attn_ln_g"].reshape(KC, 128).T),
        "lnb_t": f(inputs["L0_attn_ln_b"].reshape(KC, 128).T),
        "mur_t": f(inputs["L0_ffn_mu_r"].reshape(KC, 128).T),
        "muk_t": f(inputs["L0_ffn_mu_k"].reshape(KC, 128).T),
    }
    x = inputs["L0_attn_x"]
    return [dict(shared, x_t=f(x[b].T)) for b in range(8)]


def kernel(**inputs):
    nc = build_program()
    in_maps = _prep_inputs(inputs)
    res = run_bass_kernel_spmd(nc, in_maps, core_ids=list(range(8)))
    return np.stack([np.ascontiguousarray(r["y_t"].T) for r in res.results])


# version 33 (final)
# speedup vs baseline: 1.0105x; 1.0104x over previous
import numpy as np

import concourse.bass as bass
import concourse.mybir as mybir
from concourse.bass_utils import run_bass_kernel_spmd

F32 = mybir.dt.float32
F16 = mybir.dt.float16
BF16 = mybir.dt.bfloat16
U8 = mybir.dt.uint8
AF = mybir.ActivationFunctionType
ALU = mybir.AluOpType

DM, DI, NS, DR = 1024, 2048, 16, 64
T, TH, NJ = 2048, 1024, 512
KC, DC, GRP = 8, 16, 4
EPS = 1e-5
TAPS = []
EMIT_LOG = None
ESZ = {F32: 4, F16: 2, BF16: 2, U8: 1}


class Mem:
    def __init__(self, total, start=0):
        self.total, self.cur = total, start

    def alloc(self, name, cols, dt_):
        off = -(-self.cur // 32) * 32
        self.cur = off + cols * ESZ[dt_]
        assert self.cur <= self.total, f"SBUF overflow at {name}: {self.cur}"
        return off


def build_program():
    nc = bass.Bass("TRN2", target_bir_lowering=False)

    dx = nc.dram_tensor("x_t", [DM, T], F32, kind="ExternalInput").ap()
    dWin = nc.dram_tensor("w_in_t", [DM, 4192], F32, kind="ExternalInput").ap()
    dWdt = nc.dram_tensor("w_dt_t", [DR, DI], F32, kind="ExternalInput").ap()
    dWout = nc.dram_tensor("w_out_t", [DI, DM], F32, kind="ExternalInput").ap()
    dWr = nc.dram_tensor("w_r_t", [DM, DM], F32, kind="ExternalInput").ap()
    dWk = nc.dram_tensor("w_k_t", [DM, DM], F32, kind="ExternalInput").ap()
    dWv = nc.dram_tensor("w_v_t", [DM, DM], F32, kind="ExternalInput").ap()
    dAlog = nc.dram_tensor("a_log_t", [128, DC * NS], F32, kind="ExternalInput").ap()
    dD = nc.dram_tensor("dd_t", [128, DC], F32, kind="ExternalInput").ap()
    dbdt = nc.dram_tensor("bdt_t", [128, DC], F32, kind="ExternalInput").ap()
    dcw = nc.dram_tensor("convw_t", [128, DC * 4], F32, kind="ExternalInput").ap()
    dcb = nc.dram_tensor("convb_t", [128, DC], F32, kind="ExternalInput").ap()
    dlng = nc.dram_tensor("lng_t", [128, KC], F32, kind="ExternalInput").ap()
    dlnb = nc.dram_tensor("lnb_t", [128, KC], F32, kind="ExternalInput").ap()
    dmur = nc.dram_tensor("mur_t", [128, KC], F32, kind="ExternalInput").ap()
    dmuk = nc.dram_tensor("muk_t", [128, KC], F32, kind="ExternalInput").ap()
    dy = nc.dram_tensor("y_t", [DM, T], F32, kind="ExternalOutput").ap()

    sWin = nc.dram_tensor("s_win", [DM, 4192], BF16, kind="Internal").ap()
    sWout = nc.dram_tensor("s_wout", [DI, DM], BF16, kind="Internal").ap()
    sWr = nc.dram_tensor("s_wr", [DM, DM], BF16, kind="Internal").ap()
    sWk = nc.dram_tensor("s_wk", [DM, DM], BF16, kind="Internal").ap()
    sWv = nc.dram_tensor("s_wv", [DM, DM], BF16, kind="Internal").ap()
    sXbf = nc.dram_tensor("s_xbf", [DM, T], BF16, kind="Internal").ap()
    sBC = nc.dram_tensor("s_bc", [32, TH], F16, kind="Internal").ap()
    sDiag = nc.dram_tensor("s_diag", [DC * 4 * 128, 128], BF16, kind="Internal").ap()

    TOT = 206 * 1024
    mem = Mem(TOT)
    o_yg = mem.alloc("y_g", DC * TH, BF16)
    o_A = mem.alloc("A", DC * NS, F32)
    o_D = mem.alloc("D", DC, F32)
    o_bdt = mem.alloc("bdt", DC, F32)
    o_cw = mem.alloc("convw", DC * 4, F32)
    o_cwb = mem.alloc("convwb", DC * 4, BF16)
    o_cb = mem.alloc("convb", DC, F32)
    o_lng = mem.alloc("lng", KC, F32)
    o_lnb = mem.alloc("lnb", KC, F32)
    o_mur = mem.alloc("mur", KC, F32)
    o_muk = mem.alloc("muk", KC, F32)
    o_onesb = mem.alloc("onesb", 1, BF16)
    o_bcl = mem.alloc("bcl", 128, F32)
    o_carry = mem.alloc("carry", DC * NS, F32)
    o_wdt = mem.alloc("wdt_bf", DI, BF16)
    o_dtin = mem.alloc("dtin_bf", TH, BF16)
    o_wbcd = mem.alloc("wbcd", KC * 96, BF16)
    o_x1c = mem.alloc("x1_carry", KC, F32)
    o_xsc = mem.alloc("xs_carry", DC * 4, F16)
    o_bcsb = mem.alloc("bc_sb", TH, F16)
    o_row1 = mem.alloc("row_1", NJ, F32)
    o_row2 = mem.alloc("row_2", NJ, F32)
    o_row3 = mem.alloc("row_3", NJ, F32)
    o_ceps = mem.alloc("c_eps", 1, F32)
    o_cmh = mem.alloc("c_mh", 1, F32)
    o_cm1 = mem.alloc("c_m1", 1, F32)
    o_c1 = mem.alloc("c_1", 1, F32)
    PERS = mem.cur

    m2 = Mem(TOT, PERS)
    o_dt = [m2.alloc(f"dt{i}", TH, F16) for i in range(2 * GRP)]
    o_u = [m2.alloc(f"u{i}", TH, F16) for i in range(2 * GRP)]
    o_dtu = [m2.alloc(f"dtu{i}", TH, F16) for i in range(2 * GRP)]
    o_sz = [m2.alloc(f"sz{i}", TH, F16) for i in range(2 * GRP)]
    o_acc = [m2.alloc(f"acc{i}", TH, F16) for i in range(GRP)]
    o_da = [m2.alloc(f"da{i}", TH, F16) for i in range(6)]
    o_bm = [m2.alloc(f"bm{i}", TH, F16) for i in range(1)]
    o_ptmp = o_bm[0]
    o_yfin = o_bm[0]
    o_hh = [m2.alloc(f"h{i}", TH, F16) for i in range(4)]
    o_ymE = [m2.alloc(f"ymE{i}", TH, F16) for i in range(GRP)]
    o_ymT = m2.alloc("ymT", TH, F16)
    o_brep = [m2.alloc(f"brep{i}", TH, F16) for i in range(2)]
    o_crep = [m2.alloc(f"crep{i}", TH, F16) for i in range(2)]
    o_xs = [m2.alloc(f"xs{i}", TH + 4, BF16) for i in range(2)]
    o_wzx = [m2.alloc(f"wzx{i}", 2 * KC * 128, BF16) for i in range(2)]
    o_dg = [m2.alloc(f"dg{i}", 4 * 128, BF16) for i in range(2)]
    o_sp1 = m2.alloc("sp1", NJ, F32)
    o_sp2 = [m2.alloc(f"sp2{i}", NJ, F16) for i in range(2)]
    o_sp3 = [m2.alloc(f"sp3{i}", NJ, F16) for i in range(2)]
    o_xbh = m2.alloc("x_bh", KC * TH, BF16)
    S2_END = m2.cur

    m3 = Mem(TOT, PERS)
    o_sa_ = m3.alloc("streamA", KC * NJ, F32)
    o_x1 = m3.alloc("x1", KC * NJ, F32)
    o_vbf = m3.alloc("vbf", KC * NJ, BF16)
    o_sqbf = m3.alloc("sqbf", KC * NJ, BF16)
    o_rbf = m3.alloc("rbf", KC * NJ, BF16)
    o_k2bf = m3.alloc("k2bf", KC * NJ, BF16)
    o_xr = m3.alloc("xr", KC * NJ, BF16)
    o_xk = m3.alloc("xk", KC * NJ, BF16)
    o_kr = m3.alloc("kr", NJ, BF16)
    o_zt = m3.alloc("ztmp", NJ, F32)
    o_x1e = m3.alloc("x1e", NJ + 1, F32)
    o_xfull = m3.alloc("xfull", KC * NJ, F32)
    o_wst = [m3.alloc(f"wst{i}", 16 * 128, BF16) for i in range(2)]
    S3_END = m3.cur

    mt = Mem(TOT, 0)
    mt.cur = 0
    m0 = Mem(TOT, PERS)
    o_stg2 = [m0.alloc(f"stg{i}", 4192, F32) for i in range(2)]
    o_stgb2 = [m0.alloc(f"stgb{i}", 4192, BF16) for i in range(2)]
    o_stg = o_stg2[0]
    o_stgb = o_stgb2[0]
    S0_END = m0.cur
    mt.cur = max(S2_END, S3_END, S0_END)
    o_stg3 = [mt.alloc(f"stg3_{i}", 512, F32) for i in range(2)]
    o_stgb3 = [mt.alloc(f"stgb3_{i}", 512, BF16) for i in range(2)]
    assert mt.cur <= TOT, (S2_END, S3_END, S0_END, mt.cur)

    sched = []

    with (
        nc.sbuf_tensor("big", [128, TOT], U8) as big,
        nc.psum_tensor("ps_a", [128, 512], F32) as ps_a,
        nc.psum_tensor("ps_b", [128, 512], F32) as ps_b,
        nc.psum_tensor("ps_c", [128, 512], F32) as ps_c,
        nc.psum_tensor("ps_d", [128, 512], F32) as ps_d,
        nc.psum_tensor("ps_e", [128, 512], F32) as ps_e,
        nc.psum_tensor("ps_f", [128, 512], F32) as ps_f,
        nc.psum_tensor("ps_g", [128, 512], F32) as ps_g,
        nc.psum_tensor("ps_h", [128, 512], F32) as ps_h,
        nc.semaphore("sd") as sd,
        nc.semaphore("sg") as sg,
        nc.semaphore("spe") as spe,
        nc.semaphore("sa") as sa,
        nc.semaphore("sv") as sv,
        nc.semaphore("sp") as sp,
        nc.semaphore("sc") as sc,
        nc.Block() as blk,
    ):
        SEM = {"sd": sd, "sg": sg, "spe": spe, "sa": sa, "sv": sv, "sp": sp,
               "sc": sc}
        CNT = {k: 0 for k in SEM}
        STEP = {"sd": 16, "sg": 16, "spe": 1, "sa": 1, "sv": 1, "sp": 1,
                "sc": 16}
        ENG_SEM = {"sync": "sd", "gpsimd_dma": "sg", "tensor": "spe",
                   "scalar": "sa", "vector": "sv", "gpsimd": "sp",
                   "scalar_dma": "sc"}

        def V(off, cols, dt_, p0=0, p1=128):
            return big[p0:p1, off:off + cols * ESZ[dt_]].bitcast(dt_)

        def R1(off, p0=0):
            return V(off, NJ, F32, p0, p0 + 1)

        res = {}

        def _wait_list(reads, writes):
            need = {}
            for nm in reads:
                st = res.get(nm)
                if st and st["w"]:
                    s, v = st["w"]
                    need[s] = max(need.get(s, 0), v)
            for nm in writes:
                st = res.get(nm)
                if st:
                    if st["w"]:
                        s, v = st["w"]
                        need[s] = max(need.get(s, 0), v)
                    for s, v in st["r"]:
                        need[s] = max(need.get(s, 0), v)
            return sorted(need.items())

        def emit(eng, body, reads=(), writes=(), extra=(), last=True):
            sem_name = ENG_SEM[eng]
            waits = _wait_list(reads, writes)
            extra = [(s, v) for s, v in extra if v > 0]
            if EMIT_LOG is not None:
                EMIT_LOG.append((eng, ENG_SEM[eng], CNT[ENG_SEM[eng]] + STEP[ENG_SEM[eng]],
                                 tuple(waits), tuple(reads), tuple(writes)))
            CNT[sem_name] += STEP[sem_name]
            val = CNT[sem_name]
            eng_stream = {"gpsimd_dma": "gpsimd",
                          "scalar_dma": "scalar"}.get(eng, eng)

            def fn(e, body=body, waits=tuple(waits), extra=tuple(extra),
                   sem_name=sem_name):
                for s, v in waits:
                    e.wait_ge(SEM[s], v)
                for s, v in extra:
                    e.wait_ge(SEM[s], v)
                r = body(e)
                inc = STEP[sem_name]
                r.then_inc(SEM[sem_name], inc)
            sched.append((eng_stream, fn))
            ev = (sem_name, val)
            for nm in writes:
                res[nm] = {"w": ev, "r": []}
            for nm in reads:
                st = res.setdefault(nm, {"w": None, "r": []})
                st["r"].append(ev)
            return ev

        def raw_wait(eng, ev):
            def fn(e, ev=ev):
                e.wait_ge(SEM[ev[0]], ev[1])
            sched.append((eng, fn))

        tap_ap = {}

        def tap(name, view, reads):
            if name not in TAPS:
                return
            if name not in tap_ap:
                tap_ap[name] = nc.dram_tensor(
                    "tap_" + name, list(view.shape), view.dtype,
                    kind="ExternalOutput").ap()
            emit("sync", lambda e, view=view, name=name: e.dma_start(
                out=tap_ap[name], in_=view), reads=list(reads))

        OVL = (["stg", "stgb", "stgx0", "stgx1"]
               + [f"dt{i}_{j}" for i in range(2 * GRP) for j in range(2)]
               + [f"u{i}_{j}" for i in range(2 * GRP) for j in range(2)]
               + [f"sz{i}_{j}" for i in range(2 * GRP) for j in range(2)]
               + [f"dtu{i}" for i in range(2 * GRP)]
               + [f"accT{i}" for i in range(GRP)]
               + [f"ymE{i}" for i in range(GRP)] + ["ymT"]
               + [f"da{i}" for i in range(4)] + ["bm0"]
               + [f"hh{i}" for i in range(2)]
               + [f"brep{i}" for i in range(2)] + [f"crep{i}" for i in range(2)]
               + [f"xsD{b}{j}" for b in range(2) for j in range(2)]
               + [f"xsP{b}" for b in range(2)]
               + [f"wzxA{b}" for b in range(2)] + [f"wzxB{b}" for b in range(2)]
               + [f"dg{b}" for b in range(2)]
               + ["sp1", "sp20", "sp21", "sp30", "sp31", "yfin", "ptmp"]
               + [f"sa{o}" for o in range(KC)] + ["x1b"]
               + ["vbf", "sqbf", "kr", "zt", "x1e", "x1e0"]
               + [f"rbf{o}" for o in range(KC)] + [f"k2{o}" for o in range(KC)]
               + [f"xr{k}" for k in range(KC)] + [f"xk{k}" for k in range(KC)]
               + ["xfull"] + [f"wst{i}" for i in range(2)])

        def overlay_barrier():
            need = {}
            for nm in OVL:
                st = res.get(nm)
                if not st:
                    continue
                evs = ([st["w"]] if st["w"] else []) + list(st["r"])
                for s, v in evs:
                    need[s] = max(need.get(s, 0), v)
            evs = sorted(need.items())
            for nm in OVL:
                res[nm] = {"w": None, "r": list(evs)}

        emit("sync", lambda e: e.dma_start(out=V(o_A, DC * NS, F32), in_=dAlog),
             writes=["A"])
        for dr_, off, cols, nm in [
            (dD, o_D, DC, "D"), (dbdt, o_bdt, DC, "bdt"),
            (dcw, o_cw, DC * 4, "cw"), (dcb, o_cb, DC, "cb"),
            (dlng, o_lng, KC, "lng"), (dlnb, o_lnb, KC, "lnb"),
            (dmur, o_mur, KC, "mur"), (dmuk, o_muk, KC, "muk"),
        ]:
            emit("sync", lambda e, dr_=dr_, off=off, cols=cols: e.dma_start(
                out=V(off, cols, F32), in_=dr_), writes=[nm])
        emit("scalar", lambda e: e.activation(
            out=V(o_A, DC * NS, F32), in_=V(o_A, DC * NS, F32), func=AF.Exp),
            reads=["A"], writes=["A"])
        emit("vector", lambda e: e.tensor_scalar_mul(
            V(o_A, DC * NS, F32), V(o_A, DC * NS, F32), -1.0),
            reads=["A"], writes=["A"])
        emit("vector", lambda e: e.memset(V(o_onesb, 1, BF16), 1.0),
             writes=["onesb"])
        emit("vector", lambda e: e.memset(V(o_ceps, 1, F32), EPS), writes=["ceps"])
        emit("vector", lambda e: e.memset(V(o_cmh, 1, F32), -0.5), writes=["cmh"])
        emit("vector", lambda e: e.memset(V(o_cm1, 1, F32), -1.0), writes=["cm1"])
        emit("vector", lambda e: e.memset(V(o_c1, 1, F32), 1.0), writes=["c1"])
        emit("vector", lambda e: e.memset(V(o_bcl, 128, F32, 0, 1), 1.0),
             writes=["bcl"])
        emit("vector", lambda e: e.memset(V(o_x1c, KC, F32), 0.0),
             writes=["x1c"])
        emit("vector", lambda e: e.memset(V(o_xsc, DC * 4, F16), 0.0),
             writes=[f"xsc{dc_}" for dc_ in range(DC)])
        emit("vector", lambda e: e.tensor_copy(
            out=V(o_cwb, DC * 4, BF16), in_=V(o_cw, DC * 4, F32)),
            reads=["cw"], writes=["cwb"])

        emit("sync", lambda e: e.dma_start(
            out=V(o_stg2[0], KC * 96, F32)
            .rearrange("p (k c) -> p k c", c=96)[:, :, 0:64],
            in_=dWin[0:DM, 4128:4192].rearrange("(k p) c -> p k c", p=128)),
            writes=["stg0"])
        emit("sync", lambda e: e.dma_start(
            out=V(o_stg2[0], KC * 96, F32)
            .rearrange("p (k c) -> p k c", c=96)[:, :, 64:96],
            in_=dWin[0:DM, 4096:4128].rearrange("(k p) c -> p k c", p=128)),
            writes=["stg0"])
        emit("scalar", lambda e: e.activation(
            out=V(o_wbcd, KC * 96, BF16), in_=V(o_stg2[0], KC * 96, F32),
            func=AF.Copy),
            reads=["stg0"], writes=["wbcd"])
        emit("sync", lambda e: e.dma_start(
            out=V(o_stg2[1], DI, F32, 0, 64), in_=dWdt),
            writes=["stg1"])
        emit("vector", lambda e: e.tensor_copy(
            out=V(o_wdt, DI, BF16, 0, 64), in_=V(o_stg2[1], DI, F32, 0, 64)),
            reads=["stg1"], writes=["wdt"])
        for r in range(DM // 128):
            p = r % 2
            emit("sync", lambda e, r=r, p=p: e.dma_start(
                out=V(o_stg2[p], TH, F32),
                in_=dx[r * 128:(r + 1) * 128, 0:TH]),
                writes=[f"stg{p}"])
            emit("vector", lambda e, r=r, p=p: e.tensor_copy(
                out=V(o_xbh + r * TH * 2, TH, BF16),
                in_=V(o_stg2[p], TH, F32)),
                reads=[f"stg{p}"], writes=["x_bh"])
        for ci, (c0_, tags_) in enumerate([
                (0, ["sWz0", "sWz1", "sWz2", "sWz3"]),
                (2048, ["sWx0", "sWx1", "sWx2", "sWx3"])]):
            p = ci % 2
            emit("scalar_dma", lambda e, c0=c0_, p=p: e.dma_start(
                out=V(o_stg2[p], KC * 512, F32)
                .rearrange("p (k c) -> p k c", c=512),
                in_=dWin[0:DM, c0:c0 + 512]
                .rearrange("(k p) c -> p k c", p=128)),
                writes=[f"stg{p}"])
            emit("vector", lambda e, p=p: e.tensor_copy(
                out=V(o_stgb2[p], KC * 512, BF16),
                in_=V(o_stg2[p], KC * 512, F32)),
                reads=[f"stg{p}"], writes=[f"stgb{p}"])
            emit("gpsimd_dma", lambda e, c0=c0_, p=p: e.dma_start(
                out=sWin[0:DM, c0:c0 + 512]
                .rearrange("(k p) c -> p k c", p=128),
                in_=V(o_stgb2[p], KC * 512, BF16)
                .rearrange("p (k c) -> p k c", c=512)),
                reads=[f"stgb{p}"], writes=tags_)

        late_jobs = []
        lc_ctr = [0]

        def late_stg3(load_fn, store_fn, tags):
            def job(load_fn=load_fn, store_fn=store_fn, tags=tags):
                p = lc_ctr[0] % 2
                lc_ctr[0] += 1
                emit("scalar_dma", lambda e, p=p: load_fn(e, p),
                     writes=[f"stg3_{p}"])
                emit("scalar", lambda e, p=p: e.activation(
                    out=V(o_stgb3[p], 512, BF16),
                    in_=V(o_stg3[p], 512, F32), func=AF.Copy),
                    reads=[f"stg3_{p}"], writes=[f"stgb3_{p}"])
                emit("gpsimd_dma", lambda e, p=p: store_fn(e, p),
                     reads=[f"stgb3_{p}"], writes=tags)
            late_jobs.append(job)

        for dc_ in range(GRP, DC):
            for base, tname in ((0, "sWz"), (2048, "sWx")):
                for cb in range(2):
                    c0_ = base + dc_ * 128 + cb * 64
                    late_stg3(
                        lambda e, p, c0=c0_: e.dma_start(
                            out=V(o_stg3[p], 512, F32)
                            .rearrange("p (k c) -> p k c", c=64),
                            in_=dWin[0:DM, c0:c0 + 64]
                            .rearrange("(k p) c -> p k c", p=128)),
                        lambda e, p, c0=c0_: e.dma_start(
                            out=sWin[0:DM, c0:c0 + 64]
                            .rearrange("(k p) c -> p k c", p=128),
                            in_=V(o_stgb3[p], 512, BF16)
                            .rearrange("p (k c) -> p k c", c=64)),
                        [f"{tname}{dc_}"] if cb == 1 else [])
        for r in range(DM // 128):
            for cb in range(2):
                c0_ = TH + cb * 512
                late_stg3(
                    lambda e, p, r=r, c0=c0_: e.dma_start(
                        out=V(o_stg3[p], 512, F32),
                        in_=dx[r * 128:(r + 1) * 128, c0:c0 + 512]),
                    lambda e, p, r=r, c0=c0_: e.dma_start(
                        out=sXbf[r * 128:(r + 1) * 128, c0:c0 + 512],
                        in_=V(o_stgb3[p], 512, BF16)),
                    ["sXbf1"] if (r == DM // 128 - 1 and cb == 1) else [])

        def late_cast(dsrc, ddst, rows, tag):
            nchunks = rows // 128
            for r in range(nchunks):
                for cb in range(2):
                    late_stg3(
                        lambda e, p, r=r, cb=cb, dsrc=dsrc: e.dma_start(
                            out=V(o_stg3[p], 512, F32),
                            in_=dsrc[r * 128:(r + 1) * 128,
                                     cb * 512:(cb + 1) * 512]),
                        lambda e, p, r=r, cb=cb, ddst=ddst: e.dma_start(
                            out=ddst[r * 128:(r + 1) * 128,
                                     cb * 512:(cb + 1) * 512],
                            in_=V(o_stgb3[p], 512, BF16)),
                        [tag] if (r == nchunks - 1 and cb == 1) else [])

        late_cast(dWout, sWout, DI, "sWout")
        late_cast(dWr, sWr, DM, "sWr")
        late_cast(dWk, sWk, DM, "sWk")
        late_cast(dWv, sWv, DM, "sWv")

        emit("vector", lambda e: e.memset(V(o_stgb2[0], 4096, BF16), 0.0),
             writes=["stgb0"])
        for i in range(2):
            emit("sync", lambda e, i=i: e.dma_start(
                out=sDiag[i * 4096:(i + 1) * 4096, :]
                .rearrange("(p a) c -> p (a c)", p=128),
                in_=V(o_stgb2[0], 4096, BF16)),
                reads=["stgb0"], writes=[f"diagz{i}"])
        sDiag_f = sDiag.rearrange("a b -> (a b)")
        for idx in range(DC * 4):
            dst = sDiag_f[idx * 16384:(idx + 1) * 16384][0:16384:129]
            emit("sync", lambda e, dst=dst, idx=idx: e.dma_start(
                out=dst, in_=V(o_cwb, DC * 4, BF16)[:, idx:idx + 1],
                allow_slow_non_contiguous=True),
                reads=["cwb", "diagz0", "diagz1"],
                writes=["diag"] if idx == DC * 4 - 1 else [])

        PS = {"a": ps_a, "b": ps_b, "c": ps_c, "d": ps_d, "e": ps_e,
              "f": ps_f, "g": ps_g, "h": ps_h}

        def mm(e, psm, lhsT, rhs, start, stop):
            return e.matmul(psm, lhsT, rhs, start=start, stop=stop)

        for h in range(2):
            hb = h * TH
            overlay_barrier()

            def emit_s1(h_, hb_):
                if h_ == 1:
                    emit("sync", lambda e, hb=hb_: e.dma_start(
                        out=V(o_xbh, KC * TH, BF16)
                        .rearrange("p (k t) -> p k t", t=TH),
                        in_=sXbf[0:DM, hb:hb + TH]
                        .rearrange("(k p) t -> p k t", p=128)),
                        reads=["sXbf1"], writes=["x_bh"])
                for j in range(2):
                    pnm = "ps_" + ("a" if j == 0 else "b")
                    psm = PS[pnm[3]]

                    def mm_bcd(e, j=j, psm=psm):
                        last = None
                        for kc in range(KC):
                            last = mm(e, psm[0:96, :],
                                      V(o_wbcd + kc * 96 * 2, 96, BF16),
                                      V(o_xbh + kc * TH * 2 + (j * NJ) * 2,
                                        NJ, BF16),
                                      kc == 0, kc == KC - 1)
                        return last
                    emit("tensor", mm_bcd,
                         reads=["wbcd", "x_bh"],
                         writes=[pnm])
                    emit("scalar", lambda e, j=j, psm=psm: e.activation(
                        out=V(o_dtin + j * NJ * 2, NJ, BF16, 0, 64),
                        in_=psm[0:64, :], func=AF.Copy),
                        reads=[pnm], writes=[f"dtin{j}"])
                    emit("scalar", lambda e, j=j, psm=psm: e.activation(
                        out=V(o_bcsb + j * NJ * 2, NJ, F16, 64, 96),
                        in_=psm[64:96, :], func=AF.Copy),
                        reads=[pnm], writes=[f"bcsb{j}"])
                emit("sync", lambda e: e.dma_start(
                    out=sBC, in_=V(o_bcsb, TH, F16, 64, 96)),
                    reads=["bcsb0", "bcsb1"], writes=["sBC"])
            if h == 0:
                emit_s1(0, 0)

            unit = [0]
            ev_scan, ev_ym = {}, {}
            NGRP = DC // GRP
            usched = [((g_ % 2) * GRP + i0_, g_ * GRP + i0_, n_)
                      for g_ in range(NGRP) for n_ in range(NS)
                      for i0_ in range(GRP)]
            nxe = [0]

            def drain_exps(upto):
                while nxe[0] <= upto and nxe[0] < len(usched):
                    ue = nxe[0]
                    nxe[0] += 1
                    ie, dce, ne = usched[ue]
                    emit("scalar", lambda e, i=ie, ub=ue % 6, dc=dce,
                         n=ne: e.activation(
                             out=V(o_da[ub], TH, F16),
                             in_=V(o_dt[i], TH, F16), func=AF.Exp,
                             scale=V(o_A, DC * NS, F32)
                             [:, dc * NS + n:dc * NS + n + 1]),
                         reads=[f"dt{i}_0", f"dt{i}_1", "A"],
                         writes=[f"da{ue % 6}"])

            def emit_prod(g, only_i0=None, part=None, h=h, hb=hb):
                dcs = [g * GRP + i for i in range(GRP)]
                gp = (g % 2) * GRP
                for i0, dc in enumerate(dcs):
                    if part == 2 or (only_i0 is not None and i0 != only_i0):
                        continue
                    i = gp + i0
                    bf = i0 % 2
                    emit("sync", lambda e, dc=dc, bf=bf: e.dma_start(
                        out=V(o_wzx[bf], KC * 128, BF16)
                        .rearrange("p (k c) -> p k c", c=128),
                        in_=sWin[0:DM, dc * 128:(dc + 1) * 128]
                        .rearrange("(k p) c -> p k c", p=128)),
                        reads=[f"sWz{dc}"], writes=[f"wzxA{bf}"])
                    emit("sync", lambda e, dc=dc, bf=bf: e.dma_start(
                        out=V(o_wzx[bf] + KC * 128 * 2, KC * 128, BF16)
                        .rearrange("p (k c) -> p k c", c=128),
                        in_=sWin[0:DM, 2048 + dc * 128:2048 + (dc + 1) * 128]
                        .rearrange("(k p) c -> p k c", p=128)),
                        reads=[f"sWx{dc}"], writes=[f"wzxB{bf}"])
                    emit("sync", lambda e, dc=dc, bf=bf: e.dma_start(
                        out=V(o_dg[bf], 4 * 128, BF16)
                        .rearrange("p (f c) -> p f c", c=128),
                        in_=sDiag[dc * 512:(dc + 1) * 512, :]
                        .rearrange("(f p) c -> p f c", p=128)),
                        reads=["diag"], writes=[f"dg{bf}"])
                    for j in range(2):
                        pnm = "ps_" + ("a" if j == 0 else "b")
                        psm = PS[pnm[3]]

                        def mm_xs(e, bf=bf, j=j, psm=psm):
                            last = None
                            for kc in range(KC):
                                last = mm(e, psm[:],
                                          V(o_wzx[bf] + kc * 128 * 2, 128, BF16),
                                          V(o_xbh + kc * TH * 2 + (j * NJ) * 2,
                                            NJ, BF16),
                                          kc == 0, kc == KC - 1)
                            return last
                        emit("tensor", mm_xs, reads=[f"wzxA{bf}", "x_bh"],
                             writes=[pnm])
                        emit("scalar", lambda e, bf=bf, j=j, psm=psm: e.activation(
                            out=V(o_xs[bf] + (3 + j * NJ) * 2, NJ, BF16),
                            in_=psm[:], func=AF.Copy),
                            reads=[pnm], writes=[f"xsD{bf}{j}"])
                    emit("scalar", lambda e, bf=bf, dc=dc: e.activation(
                        out=V(o_xs[bf], 3, BF16),
                        in_=V(o_xsc + dc * 4 * 2, 3, BF16), func=AF.Copy),
                        reads=[f"xsc{dc}"], writes=[f"xsP{bf}"])
                    if h == 0:
                        emit("scalar", lambda e, bf=bf, dc=dc: e.activation(
                            out=V(o_xsc + dc * 4 * 2, 3, BF16),
                            in_=V(o_xs[bf] + (3 + TH - 3) * 2, 3, BF16),
                            func=AF.Copy),
                            reads=[f"xsD{bf}1"], writes=[f"xsc{dc}"])
                    for j in range(2):
                        pnm = "ps_" + ("c" if j == 0 else "d")
                        psm = PS[pnm[3]]

                        def mm_cv(e, bf=bf, j=j, psm=psm):
                            last = None
                            for tp in range(4):
                                last = mm(e, psm[:],
                                          V(o_dg[bf] + tp * 128 * 2, 128, BF16),
                                          V(o_xs[bf] + (j * NJ + tp) * 2, NJ, BF16),
                                          tp == 0, tp == 3)
                            return last
                        emit("tensor", mm_cv,
                             reads=[f"dg{bf}", f"xsD{bf}0", f"xsD{bf}1",
                                    f"xsP{bf}"],
                             writes=[pnm])
                        emit("scalar", lambda e, i=i, j=j, psm=psm, dc=dc:
                             e.activation(
                                 out=V(o_u[i] + j * NJ * 2, NJ, F16),
                                 in_=psm[:], func=AF.Silu,
                                 bias=V(o_cb, DC, F32)[:, dc:dc + 1]),
                             reads=[pnm, "cb"], writes=[f"u{i}_{j}"])
                    for j in range(2):
                        pnm = "ps_" + ("e" if j == 0 else "f")
                        psm = PS[pnm[3]]

                        def mm_z(e, bf=bf, j=j, psm=psm):
                            last = None
                            for kc in range(KC):
                                last = mm(e, psm[:],
                                          V(o_wzx[bf] + (KC + kc) * 128 * 2, 128,
                                            BF16),
                                          V(o_xbh + kc * TH * 2 + (j * NJ) * 2,
                                            NJ, BF16),
                                          kc == 0, kc == KC - 1)
                            return last
                        emit("tensor", mm_z, reads=[f"wzxB{bf}", "x_bh"],
                             writes=[pnm])
                        emit("scalar", lambda e, i=i, j=j, psm=psm: e.activation(
                            out=V(o_sz[i] + j * NJ * 2, NJ, F16),
                            in_=psm[:], func=AF.Silu),
                            reads=[pnm], writes=[f"sz{i}_{j}"])
                for i0, dc in enumerate(dcs):
                    if part == 1:
                        continue
                    i = gp + i0
                    for j in range(2):
                        pnm = "ps_" + ("g" if (i0 % 2) == 0 else "h")
                        psm = PS[pnm[3]]
                        sj = i0 % 2
                        emit("tensor", lambda e, dc=dc, j=j, psm=psm: mm(
                            e, psm[0:128, j * 0:512] if False else psm[:],
                            V(o_wdt + dc * 128 * 2, 128, BF16, 0, 64),
                            V(o_dtin + j * NJ * 2, NJ, BF16, 0, 64),
                            True, True),
                            reads=["wdt", f"dtin{j}"], writes=[pnm])
                        emit("scalar", lambda e, psm=psm, dc=dc: e.activation(
                            out=V(o_sp1, NJ, F32), in_=psm[:], func=AF.Abs,
                            bias=V(o_bdt, DC, F32)[:, dc:dc + 1]),
                            reads=[pnm, "bdt"], writes=["sp1"])
                        emit("scalar", lambda e, sj=sj: e.activation(
                            out=V(o_sp2[sj], NJ, F16), in_=V(o_sp1, NJ, F32),
                            func=AF.Exp, scale=V(o_cm1, 1, F32)),
                            reads=["sp1", "cm1"], writes=[f"sp2{sj}"])
                        emit("scalar", lambda e, sj=sj: e.activation(
                            out=V(o_sp3[sj], NJ, F16), in_=V(o_sp2[sj], NJ, F16),
                            func=AF.Ln, bias=V(o_c1, 1, F32)),
                            reads=[f"sp2{sj}", "c1"], writes=[f"sp3{sj}"])
                        emit("scalar", lambda e, psm=psm, dc=dc, sj=sj: e.activation(
                            out=V(o_sp2[sj], NJ, F16), in_=psm[:], func=AF.Relu,
                            bias=V(o_bdt, DC, F32)[:, dc:dc + 1]),
                            reads=[pnm, "bdt"], writes=[f"sp2{sj}"])
                        emit("vector", lambda e, i=i, j=j, sj=sj: e.tensor_tensor(
                            V(o_dt[i] + j * NJ * 2, NJ, F16),
                            V(o_sp3[sj], NJ, F16), V(o_sp2[sj], NJ, F16), ALU.add),
                            reads=[f"sp2{sj}", f"sp3{sj}"], writes=[f"dt{i}_{j}"])
                    emit("vector", lambda e, i=i: e.tensor_tensor(
                        V(o_dtu[i], TH, F16), V(o_dt[i], TH, F16),
                        V(o_u[i], TH, F16), ALU.mult),
                        reads=[f"dt{i}_0", f"dt{i}_1", f"u{i}_0", f"u{i}_1"],
                        writes=[f"dtu{i}"])
                    tap(f"xs_h{h}_dc{dc}", V(o_xs[i0 % 2], TH + 4, BF16),
                        [f"xsD{i0 % 2}0", f"xsD{i0 % 2}1", f"xsP{i0 % 2}"])
                    tap(f"u_h{h}_dc{dc}", V(o_u[i], TH, F16),
                        [f"u{i}_0", f"u{i}_1"])
                    tap(f"sz_h{h}_dc{dc}", V(o_sz[i], TH, F16),
                        [f"sz{i}_0", f"sz{i}_1"])
                    tap(f"dt_h{h}_dc{dc}", V(o_dt[i], TH, F16),
                        [f"dt{i}_0", f"dt{i}_1"])
                    tap(f"dtu_h{h}_dc{dc}", V(o_dtu[i], TH, F16), [f"dtu{i}"])


            emit_prod(0)
            for g in range(NGRP):
                dcs = [g * GRP + i for i in range(GRP)]
                gp = (g % 2) * GRP
                for n in range(NS):
                    pb = n % 2
                    emit("sync", lambda e, pb=pb, n=n: e.dma_start(
                        out=V(o_brep[pb], TH, F16),
                        in_=sBC[n:n + 1, :].broadcast_to([128, TH])),
                        reads=["sBC"], writes=[f"brep{pb}"])
                    emit("sync", lambda e, pb=pb, n=n: e.dma_start(
                        out=V(o_crep[pb], TH, F16),
                        in_=sBC[16 + n:17 + n, :].broadcast_to([128, TH])),
                        reads=["sBC"], writes=[f"crep{pb}"])
                    if h == 0:
                        for _ in range(3):
                            if late_jobs:
                                late_jobs.pop(0)()
                    if 8 <= n <= 11 and g + 1 < NGRP:
                        emit_prod(g + 1, only_i0=n - 8, part=1)
                    if n == 13 and g + 1 < NGRP:
                        emit_prod(g + 1, part=2)
                    for i0, dc in enumerate(dcs):
                        i = gp + i0
                        u_ = unit[0]
                        unit[0] += 1
                        drain_exps(u_ + 5)
                        ub = u_ % 6
                        emit("vector", lambda e, i=i, pb=pb:
                             e.tensor_tensor(
                                 V(o_bm[0], TH, F16), V(o_dtu[i], TH, F16),
                                 V(o_brep[pb], TH, F16), ALU.mult),
                             reads=[f"dtu{i}", f"brep{pb}"], writes=["bm0"])

                        hb_ = u_ % 4

                        def f_scan(e, ub=ub, hb_=hb_, dc=dc, n=n, h=h):
                            init = (0.0 if h == 0 else
                                    V(o_carry, DC * NS, F32)
                                    [:, dc * NS + n:dc * NS + n + 1])
                            return e.tensor_tensor_scan(
                                V(o_hh[hb_], TH, F16), V(o_da[ub], TH, F16),
                                V(o_bm[0], TH, F16), init, ALU.mult, ALU.add)
                        ev_scan[u_] = emit(
                            "vector", f_scan,
                            reads=[f"da{ub}", "bm0"] +
                                  (["carry"] if h == 1 else []),
                            writes=[f"hh{hb_}"])
                        tap(f"h_h{h}_dc{dc}_n{n}", V(o_hh[hb_], TH, F16),
                            [f"hh{hb_}"])
                        tap(f"da_h{h}_dc{dc}_n{n}", V(o_da[ub], TH, F16),
                            [f"da{ub}"])
                        tap(f"bm_h{h}_dc{dc}_n{n}", V(o_bm[0], TH, F16),
                            ["bm0"])
                        if h == 0:
                            emit("scalar", lambda e, hb_=hb_, dc=dc, n=n:
                                 e.activation(
                                     out=V(o_carry, DC * NS, F32)
                                     [:, dc * NS + n:dc * NS + n + 1],
                                     in_=V(o_hh[hb_] + (TH - 1) * 2, 1, F16),
                                     func=AF.Copy),
                                 reads=[f"hh{hb_}"], writes=["carry_el"])
                        if n % 2 == 0:
                            emit("vector", lambda e, hb_=hb_, pb=pb, i0=i0:
                                 e.tensor_tensor(
                                     V(o_ymE[i0], TH, F16), V(o_hh[hb_], TH, F16),
                                     V(o_crep[pb], TH, F16), ALU.mult),
                                 reads=[f"hh{hb_}", f"crep{pb}"],
                                 writes=[f"ymE{i0}"])
                        else:
                            emit("vector", lambda e, hb_=hb_, pb=pb:
                                 e.tensor_tensor(
                                     V(o_ymT, TH, F16), V(o_hh[hb_], TH, F16),
                                     V(o_crep[pb], TH, F16), ALU.mult),
                                 reads=[f"hh{hb_}", f"crep{pb}"], writes=["ymT"])
                            emit("vector", lambda e, i0=i0: e.tensor_tensor(
                                V(o_ptmp, TH, F16), V(o_ymE[i0], TH, F16),
                                V(o_ymT, TH, F16), ALU.add),
                                reads=[f"ymE{i0}", "ymT"], writes=["ptmp", "bm0"])
                            if n == 1:
                                emit("vector", lambda e, i0=i0: e.tensor_copy(
                                    out=V(o_acc[i0], TH, F16),
                                    in_=V(o_ptmp, TH, F16)),
                                    reads=["ptmp", "bm0"], writes=[f"accT{i0}"])
                            else:
                                emit("vector", lambda e, i0=i0: e.tensor_tensor(
                                    V(o_acc[i0], TH, F16), V(o_acc[i0], TH, F16),
                                    V(o_ptmp, TH, F16), ALU.add),
                                    reads=["ptmp", "bm0", f"accT{i0}"],
                                    writes=[f"accT{i0}"])
                for i0, dc in enumerate(dcs):
                    i = gp + i0
                    emit("vector", lambda e, i=i, i0=i0, dc=dc:
                         e.scalar_tensor_tensor(
                             V(o_yfin, TH, F16), V(o_u[i], TH, F16),
                             V(o_D, DC, F32)[:, dc:dc + 1], V(o_acc[i0], TH, F16),
                             ALU.mult, ALU.add),
                         reads=[f"accT{i0}", f"u{i}_0", f"u{i}_1", "D"],
                         writes=["yfin", "bm0"])
                    emit("vector", lambda e, i=i, dc=dc: e.tensor_tensor(
                        V(o_yg + dc * TH * 2, TH, BF16), V(o_yfin, TH, F16),
                        V(o_sz[i], TH, F16), ALU.mult),
                        reads=["yfin", "bm0", f"sz{i}_0", f"sz{i}_1"],
                        writes=[f"yg{dc}"])
                    tap(f"yg_h{h}_dc{dc}", V(o_yg + dc * TH * 2, TH, BF16),
                        [f"yg{dc}"])

            while late_jobs:
                late_jobs.pop(0)()
            if h == 0:
                emit_s1(1, TH)
            overlay_barrier()
            for j in range(2):
                jb = j * NJ
                emit("sync", lambda e, hb=hb, jb=jb: e.dma_start(
                    out=V(o_xfull, KC * NJ, F32)
                    .rearrange("p (o t) -> p o t", t=NJ),
                    in_=dx[0:DM, hb + jb:hb + jb + NJ]
                    .rearrange("(o p) t -> p o t", p=128)),
                    writes=["xfull"])
                ev_attn = {}
                for oc in range(KC):
                    wb = oc % 2
                    emit("sync", lambda e, oc=oc, wb=wb: e.dma_start(
                        out=V(o_wst[wb], 16 * 128, BF16)
                        .rearrange("p (k c) -> p k c", c=128),
                        in_=sWout[0:DI, oc * 128:(oc + 1) * 128]
                        .rearrange("(k p) c -> p k c", p=128)),
                        reads=["sWout"], writes=[f"wst{wb}"])
                    pnm = "ps_" + ("a" if oc % 2 == 0 else "b")
                    psm = PS[pnm[3]]

                    def mm_wo(e, wb=wb, psm=psm, jb=jb):
                        last = None
                        for kc in range(DC):
                            last = mm(e, psm[:],
                                      V(o_wst[wb] + kc * 128 * 2, 128, BF16),
                                      V(o_yg + kc * TH * 2 + jb * 2, NJ, BF16),
                                      kc == 0, kc == DC - 1)
                        return last
                    emit("tensor", mm_wo,
                         reads=[f"wst{wb}"] + [f"yg{d}" for d in range(DC)],
                         writes=[pnm])
                    emit("vector", lambda e, oc=oc, psm=psm:
                         e.scalar_tensor_tensor(
                             V(o_sa_ + oc * NJ * 4, NJ, F32), psm[:], 1.0,
                             V(o_xfull + oc * NJ * 4, NJ, F32),
                             ALU.mult, ALU.add),
                         reads=[pnm, "xfull"], writes=[f"sa{oc}"])

                SA = [f"sa{o}" for o in range(KC)]

                def bview(psm):
                    return psm[:].unsqueeze(1).broadcast_to([128, KC, NJ])

                def batched(off, dt_):
                    return V(off, KC * NJ, dt_).rearrange(
                        "p (o t) -> p o t", t=NJ)

                def ln_stats(tag):
                    emit("vector", lambda e: e.tensor_copy(
                        out=V(o_vbf, KC * NJ, BF16),
                        in_=V(o_sa_, KC * NJ, F32)),
                        reads=SA, writes=["vbf"])
                    emit("scalar", lambda e: e.activation(
                        out=V(o_sqbf, KC * NJ, BF16),
                        in_=V(o_sa_, KC * NJ, F32), func=AF.Square),
                        reads=SA, writes=["sqbf"])

                    def mm_sums(e):
                        last = None
                        for kc in range(KC):
                            mm(e, ps_c[0:1, :], V(o_onesb, 1, BF16),
                               V(o_vbf + kc * NJ * 2, NJ, BF16),
                               kc == 0, kc == KC - 1)
                        for kc in range(KC):
                            last = mm(e, ps_c[32:33, :], V(o_onesb, 1, BF16),
                                      V(o_sqbf + kc * NJ * 2, NJ, BF16),
                                      kc == 0, kc == KC - 1)
                        return last
                    emit("tensor", mm_sums, reads=["vbf", "sqbf", "onesb"],
                         writes=["ps_c"])
                    emit("vector", lambda e: e.tensor_scalar_mul(
                        R1(o_row1), ps_c[0:1, :], 1.0 / DM),
                        reads=["ps_c"], writes=["rowa"])
                    emit("vector", lambda e: e.tensor_tensor(
                        R1(o_row1, 32), R1(o_row1), R1(o_row1), ALU.mult),
                        reads=["rowa"], writes=["rowd"])
                    emit("vector", lambda e: e.scalar_tensor_tensor(
                        R1(o_row1, 64), ps_c[32:33, :], 1.0 / DM,
                        R1(o_row1, 32), ALU.mult, ALU.subtract),
                        reads=["ps_c", "rowd"], writes=["rowe"])
                    emit("scalar", lambda e: e.activation(
                        out=R1(o_row1, 32), in_=R1(o_row1, 64), func=AF.Ln,
                        bias=V(o_ceps, 1, F32, 0, 1)),
                        reads=["rowe", "ceps"], writes=["rowd"])
                    emit("scalar", lambda e: e.activation(
                        out=R1(o_row2), in_=R1(o_row1, 32), func=AF.Exp,
                        scale=V(o_cmh, 1, F32, 0, 1)),
                        reads=["rowd", "cmh"], writes=["rowb"])
                    emit("vector", lambda e: e.tensor_tensor(
                        R1(o_row3), R1(o_row1), R1(o_row2), ALU.mult),
                        reads=["rowa", "rowb"], writes=["rowc"])

                    def mm_bc(e):
                        mm(e, ps_d[:], V(o_bcl, 128, F32, 0, 1), R1(o_row2),
                           True, True)
                        return mm(e, ps_e[:], V(o_bcl, 128, F32, 0, 1),
                                  R1(o_row3), True, True)
                    emit("tensor", mm_bc, reads=["rowb", "rowc", "bcl"],
                         writes=["ps_d", "ps_e"])

                ln_stats("ln0")
                emit("vector", lambda e: e.tensor_tensor(
                    batched(o_x1, F32), batched(o_sa_, F32), bview(ps_d),
                    ALU.mult),
                    reads=SA + ["ps_d"], writes=["x1b"])
                emit("vector", lambda e: e.tensor_tensor(
                    batched(o_x1, F32), batched(o_x1, F32), bview(ps_e),
                    ALU.subtract),
                    reads=["x1b", "ps_e"], writes=["x1b"])
                for oc in range(KC):
                    emit("vector", lambda e, oc=oc: e.scalar_tensor_tensor(
                        V(o_sa_ + oc * NJ * 4, NJ, F32),
                        V(o_x1 + oc * NJ * 4, NJ, F32),
                        V(o_lng, KC, F32)[:, oc:oc + 1],
                        V(o_xfull + oc * NJ * 4, NJ, F32),
                        ALU.mult, ALU.add),
                        reads=["x1b", "lng", "xfull"], writes=[f"sa{oc}"])
                    emit("vector", lambda e, oc=oc: e.tensor_scalar_add(
                        V(o_sa_ + oc * NJ * 4, NJ, F32),
                        V(o_sa_ + oc * NJ * 4, NJ, F32),
                        V(o_lnb, KC, F32)[:, oc:oc + 1]),
                        reads=["lnb", f"sa{oc}"], writes=[f"sa{oc}"])
                for oc in range(KC):
                    tap(f"attnr1_h{h}_j{j}_oc{oc}",
                        V(o_sa_ + oc * NJ * 4, NJ, F32), [f"sa{oc}"])
                ln_stats("ln1")
                emit("vector", lambda e: e.tensor_tensor(
                    batched(o_x1, F32), batched(o_sa_, F32), bview(ps_d),
                    ALU.mult),
                    reads=SA + ["ps_d"], writes=["x1b"])
                emit("vector", lambda e: e.tensor_tensor(
                    batched(o_x1, F32), batched(o_x1, F32), bview(ps_e),
                    ALU.subtract),
                    reads=["x1b", "ps_e"], writes=["x1b"])
                for oc in range(KC):
                    tap(f"x1_h{h}_j{j}_oc{oc}",
                        V(o_x1 + oc * NJ * 4, NJ, F32), ["x1b"])

                for kc in range(KC):
                    emit("vector", lambda e, kc=kc: e.tensor_copy(
                        out=V(o_x1e + 4, NJ, F32),
                        in_=V(o_x1 + kc * NJ * 4, NJ, F32)),
                        reads=["x1b"], writes=["x1e"])
                    emit("vector", lambda e, kc=kc: e.tensor_copy(
                        out=V(o_x1e, 1, F32), in_=V(o_x1c + kc * 4, 1, F32)),
                        reads=["x1c"], writes=["x1e0"])
                    emit("vector", lambda e, kc=kc: e.tensor_copy(
                        out=V(o_x1c + kc * 4, 1, F32),
                        in_=V(o_x1 + kc * NJ * 4 + (NJ - 1) * 4, 1, F32)),
                        reads=["x1b"], writes=["x1c"])
                    emit("vector", lambda e: e.tensor_tensor(
                        V(o_zt, NJ, F32), V(o_x1e + 4, NJ, F32),
                        V(o_x1e, NJ, F32), ALU.subtract),
                        reads=["x1e", "x1e0"], writes=["zt"])
                    emit("vector", lambda e, kc=kc: e.scalar_tensor_tensor(
                        V(o_xr + kc * NJ * 2, NJ, BF16), V(o_zt, NJ, F32),
                        V(o_mur, KC, F32)[:, kc:kc + 1], V(o_x1e, NJ, F32),
                        ALU.mult, ALU.add),
                        reads=["zt", "mur", "x1e", "x1e0"], writes=[f"xr{kc}"])
                    emit("vector", lambda e, kc=kc: e.scalar_tensor_tensor(
                        V(o_xk + kc * NJ * 2, NJ, BF16), V(o_zt, NJ, F32),
                        V(o_muk, KC, F32)[:, kc:kc + 1], V(o_x1e, NJ, F32),
                        ALU.mult, ALU.add),
                        reads=["zt", "muk", "x1e", "x1e0"], writes=[f"xk{kc}"])

                for oc in range(KC):
                    wb = (oc // 2) % 2
                    if oc % 2 == 0:
                        emit("sync", lambda e, oc=oc, wb=wb: e.dma_start(
                            out=V(o_wst[wb], 16 * 128, BF16)
                            .rearrange("p (k m) -> p k m", m=256),
                            in_=sWr[0:DM, oc * 128:(oc + 2) * 128]
                            .rearrange("(k p) m -> p k m", p=128)),
                            reads=["sWr"], writes=[f"wst{wb}"])

                    pr = [ps_f, ps_d][oc % 2]
                    prn = ["ps_f", "ps_d"][oc % 2]

                    ho = (oc % 2) * 128 * 2

                    def mm_r(e, wb=wb, pr=pr, ho=ho):
                        last = None
                        for kc in range(KC):
                            last = mm(e, pr[:],
                                      V(o_wst[wb] + ho + kc * 256 * 2, 128, BF16),
                                      V(o_xr + kc * NJ * 2, NJ, BF16),
                                      kc == 0, kc == KC - 1)
                        return last
                    emit("tensor", mm_r,
                         reads=[f"wst{wb}"] + [f"xr{k}" for k in range(KC)],
                         writes=[prn])
                    emit("scalar", lambda e, oc=oc, pr=pr: e.activation(
                        out=V(o_rbf + oc * NJ * 2, NJ, BF16), in_=pr[:],
                        func=AF.Sigmoid),
                        reads=[prn], writes=[f"rbf{oc}"])
                for oc in range(KC):
                    wb = (oc // 2) % 2
                    if oc % 2 == 0:
                        emit("sync", lambda e, oc=oc, wb=wb: e.dma_start(
                            out=V(o_wst[wb], 16 * 128, BF16)
                            .rearrange("p (k m) -> p k m", m=256),
                            in_=sWk[0:DM, oc * 128:(oc + 2) * 128]
                            .rearrange("(k p) m -> p k m", p=128)),
                            reads=["sWk"], writes=[f"wst{wb}"])

                    pk = [ps_g, ps_e][oc % 2]
                    pkn = ["ps_g", "ps_e"][oc % 2]

                    ho = (oc % 2) * 128 * 2

                    def mm_k(e, wb=wb, pk=pk, ho=ho):
                        last = None
                        for kc in range(KC):
                            last = mm(e, pk[:],
                                      V(o_wst[wb] + ho + kc * 256 * 2, 128, BF16),
                                      V(o_xk + kc * NJ * 2, NJ, BF16),
                                      kc == 0, kc == KC - 1)
                        return last
                    emit("tensor", mm_k,
                         reads=[f"wst{wb}"] + [f"xk{k}" for k in range(KC)],
                         writes=[pkn])
                    emit("scalar", lambda e, pk=pk: e.activation(
                        out=V(o_kr, NJ, BF16), in_=pk[:], func=AF.Relu),
                        reads=[pkn], writes=["kr"])
                    emit("vector", lambda e, oc=oc: e.tensor_tensor(
                        V(o_k2bf + oc * NJ * 2, NJ, BF16), V(o_kr, NJ, BF16),
                        V(o_kr, NJ, BF16), ALU.mult),
                        reads=["kr"], writes=[f"k2{oc}"])
                for oc in range(KC):
                    wb = (oc // 2) % 2
                    if oc % 2 == 0:
                        emit("sync", lambda e, oc=oc, wb=wb: e.dma_start(
                            out=V(o_wst[wb], 16 * 128, BF16)
                            .rearrange("p (k m) -> p k m", m=256),
                            in_=sWv[0:DM, oc * 128:(oc + 2) * 128]
                            .rearrange("(k p) m -> p k m", p=128)),
                            reads=["sWv"], writes=[f"wst{wb}"])

                    pv = [ps_h, ps_g][oc % 2]
                    pvn = ["ps_h", "ps_g"][oc % 2]

                    ho = (oc % 2) * 128 * 2

                    def mm_v(e, wb=wb, pv=pv, ho=ho):
                        last = None
                        for kc in range(KC):
                            last = mm(e, pv[:],
                                      V(o_wst[wb] + ho + kc * 256 * 2, 128, BF16),
                                      V(o_k2bf + kc * NJ * 2, NJ, BF16),
                                      kc == 0, kc == KC - 1)
                        return last
                    emit("tensor", mm_v,
                         reads=[f"wst{wb}"] + [f"k2{k}" for k in range(KC)],
                         writes=[pvn])
                    emit("vector", lambda e, oc=oc, pv=pv: e.tensor_tensor(
                        V(o_zt, NJ, F32), V(o_rbf + oc * NJ * 2, NJ, BF16),
                        pv[:], ALU.mult),
                        reads=[pvn, f"rbf{oc}"], writes=["zt"])
                    emit("vector", lambda e, oc=oc: e.tensor_tensor(
                        V(o_sa_ + oc * NJ * 4, NJ, F32),
                        V(o_x1 + oc * NJ * 4, NJ, F32), V(o_zt, NJ, F32),
                        ALU.add),
                        reads=["zt", "x1b"], writes=[f"sa{oc}"])
                ln_stats("ln2")
                emit("vector", lambda e: e.tensor_tensor(
                    batched(o_x1, F32), batched(o_sa_, F32), bview(ps_d),
                    ALU.mult),
                    reads=SA + ["ps_d"], writes=["x1b"])
                emit("vector", lambda e: e.tensor_tensor(
                    batched(o_x1, F32), batched(o_x1, F32), bview(ps_e),
                    ALU.subtract),
                    reads=["x1b", "ps_e"], writes=["x1b"])
                emit("gpsimd_dma", lambda e, hb=hb, jb=jb: e.dma_start(
                    out=dy[0:DM, hb + jb:hb + jb + NJ]
                    .rearrange("(o p) t -> p o t", p=128),
                    in_=batched(o_x1, F32)),
                    reads=["x1b"], writes=[])

        def fin(e):
            for s in ("sd", "sg", "spe", "sa", "sv", "sp", "sc"):
                if CNT[s]:
                    e.wait_ge(SEM[s], CNT[s])
        sched.append(("gpsimd", fin))

        @blk.sync
        def _(eng):
            for e, fn in sched:
                if e == "sync":
                    fn(eng)

        @blk.tensor
        def _(eng):
            for e, fn in sched:
                if e == "tensor":
                    fn(eng)

        @blk.scalar
        def _(eng):
            for e, fn in sched:
                if e == "scalar":
                    fn(eng)

        @blk.vector
        def _(eng):
            for e, fn in sched:
                if e == "vector":
                    fn(eng)

        @blk.gpsimd
        def _(eng):
            for e, fn in sched:
                if e == "gpsimd":
                    fn(eng)

    mybir.codegen_inst_isa_subclasses(nc)
    return nc


def _prep_inputs(inputs):
    f = np.ascontiguousarray
    shared = {
        "w_in_t": f(inputs["L0_attn_W_in"].T),
        "w_dt_t": f(inputs["L0_attn_W_dt"].T),
        "w_out_t": f(inputs["L0_attn_W_out"].T),
        "w_r_t": f(inputs["L0_ffn_W_r"].T),
        "w_k_t": f(inputs["L0_ffn_W_k"].T),
        "w_v_t": f(inputs["L0_ffn_W_v"].T),
        "a_log_t": f(inputs["L0_attn_A_log"].reshape(DC, 128, NS)
                     .transpose(1, 0, 2).reshape(128, DC * NS)),
        "dd_t": f(inputs["L0_attn_D"].reshape(DC, 128).T),
        "bdt_t": f(inputs["L0_attn_b_dt"].reshape(DC, 128).T),
        "convw_t": f(inputs["L0_attn_conv_w"].reshape(DC, 128, 4)
                     .transpose(1, 0, 2).reshape(128, DC * 4)),
        "convb_t": f(inputs["L0_attn_conv_b"].reshape(DC, 128).T),
        "lng_t": f(inputs["L0_attn_ln_g"].reshape(KC, 128).T),
        "lnb_t": f(inputs["L0_attn_ln_b"].reshape(KC, 128).T),
        "mur_t": f(inputs["L0_ffn_mu_r"].reshape(KC, 128).T),
        "muk_t": f(inputs["L0_ffn_mu_k"].reshape(KC, 128).T),
    }
    x = inputs["L0_attn_x"]
    return [dict(shared, x_t=f(x[b].T)) for b in range(8)]


def kernel(**inputs):
    nc = build_program()
    in_maps = _prep_inputs(inputs)
    res = run_bass_kernel_spmd(nc, in_maps, core_ids=list(range(8)))
    return np.stack([np.ascontiguousarray(r["y_t"].T) for r in res.results])
